# version 48
# baseline (speedup 1.0000x reference)
import numpy as np

import concourse.bass as bass
import concourse.mybir as mybir
from concourse.bass_utils import run_bass_kernel_spmd

F32 = mybir.dt.float32
BF16 = mybir.dt.bfloat16
AF = mybir.ActivationFunctionType
ALU = mybir.AluOpType
AX = mybir.AxisListType

NEG_SENT = -1.0e30
KNOCK = -1.0e38

T_LOC = 2048
D = 1024
NST = 256
L1 = 8
NCH = T_LOC // L1
L2 = 16
NB = NCH // L2
NEXP = 32
MOE_I = 512
SH_I = 2048


class Prog:

    ENGS = ("tensor", "vector", "scalar", "sync", "gpsimd")

    def __init__(self):
        self.steps = {e: [] for e in self.ENGS}
        self.cnt = {"t": 0, "v": 0, "s": 0, "p": 0}
        self.pending = {e: None for e in self.ENGS}
        self.final_waits = []

    def _sem_of(self, eng, dma):
        assert not dma
        return {"tensor": "t", "vector": "v", "scalar": "s", "gpsimd": "p"}[eng]

    def _add(self, eng, fn, deps, dma, sem=None, fence=True):
        if not dma:
            sem = self._sem_of(eng, dma)
        self.cnt.setdefault(sem, 0)
        waits = {}
        if self.pending[eng] is not None:
            for s, v in self.pending[eng].items():
                if v > 0:
                    waits[s] = max(waits.get(s, 0), v)
            self.pending[eng] = None
        own = (self._sem_of(eng, False)
               if eng in ("tensor", "vector", "scalar") else None)
        for tok in deps:
            if tok is None:
                continue
            s, v = tok
            if s == own:
                continue
            waits[s] = max(waits.get(s, 0), v)
        if not dma and fence and eng in ("vector", "scalar", "gpsimd"):
            sem_own = self._sem_of(eng, False)
            if self.cnt[sem_own] > 0:
                waits[sem_own] = max(waits.get(sem_own, 0), self.cnt[sem_own])
        inc = 16 if dma else 1
        self.cnt[sem] += inc
        tok = (sem, self.cnt[sem])
        self.steps[eng].append((waits, fn, sem, inc))
        return tok

    def op(self, eng, fn, deps=(), fence=True):
        return self._add(eng, fn, deps, dma=False, fence=fence)

    def dma(self, eng, fn, deps=(), sem=None):
        assert sem is not None
        return self._add(eng, fn, deps, dma=True, sem=sem)

    def barrier(self, engines=None):
        snap = dict(self.cnt)
        for e in (engines if engines is not None else self.ENGS):
            prev = self.pending[e]
            merged = dict(snap)
            if prev:
                for s, v in prev.items():
                    merged[s] = max(merged.get(s, 0), v)
            self.pending[e] = merged

    def emit(self, block, sems):
        def mk(eng_name):
            def body(eng):
                waited = {}
                for waits, fn, sem, inc in self.steps[eng_name]:
                    for s, v in sorted(waits.items()):
                        if waited.get(s, -1) >= v:
                            continue
                        eng.wait_ge(sems[s], v)
                        waited[s] = v
                    fn(eng).then_inc(sems[sem], inc)
                if eng_name == "sync":
                    for s, v in self.final_waits:
                        if waited.get(s, -1) < v:
                            eng.wait_ge(sems[s], v)

            return body

        block.tensor(mk("tensor"))
        block.vector(mk("vector"))
        block.scalar(mk("scalar"))
        block.sync(mk("sync"))
        block.gpsimd(mk("gpsimd"))


class SbufAlloc:

    def __init__(self, nc, base, limit):
        self.nc = nc
        self.free_list = [(base, limit)]
        self.live = {}
        self.n = 0

    def take(self, name, shape, dtype, align=64, side="bottom"):
        esz = {F32: 4, BF16: 2}[dtype]
        per_part = int(np.prod(shape[1:])) * esz
        per_part = (per_part + align - 1) // align * align
        order = (range(len(self.free_list)) if side == "bottom"
                 else range(len(self.free_list) - 1, -1, -1))
        for i in order:
            s, e = self.free_list[i]
            if side == "bottom":
                s_al = (s + align - 1) // align * align
            else:
                s_al = (e - per_part) // align * align
                if s_al < s:
                    continue
            if s_al + per_part <= e and s_al >= s:
                self.free_list.pop(i)
                if s_al + per_part < e:
                    self.free_list.insert(i, (s_al + per_part, e))
                if s_al > s:
                    self.free_list.insert(i, (s, s_al))
                self.free_list.sort()
                self.n += 1
                h = self.nc.alloc_sbuf_tensor_at(
                    f"{name}_{self.n}", list(shape), dtype, offset=s_al)
                self.live[name] = (s_al, s_al + per_part)
                return h.ap()
        raise MemoryError(
            f"SBUF alloc failed for {name}: {per_part}B, free={self.free_list}")

    def free(self, *names):
        for name in names:
            self.free_list.append(self.live.pop(name))
        self.free_list.sort()
        out = []
        for s, e in self.free_list:
            if out and out[-1][1] >= s:
                out[-1] = (out[-1][0], max(out[-1][1], e))
            else:
                out.append([s, e])
        self.free_list = [tuple(x) for x in out]


def build_program(debug=False):
    nc = bass.Bass("TRN2", target_bir_lowering=False)

    def din(name, shape):
        return nc.dram_tensor(name, list(shape), F32, kind="ExternalInput").ap()

    X = din("X", [T_LOC, D])
    XPREV = din("XPREV", [T_LOC, D])
    FLAG = din("FLAG", [1, 1])
    W_IN = din("W_IN", [D, NST])
    W_GATE = din("W_GATE", [D, D])
    W_OUT = din("W_OUT", [D, D])
    CT = din("CT", [NST, D])
    DT = din("DT", [NST, D])
    A_N = din("A_N", [NST, NST])
    A_T = din("A_T", [NST, NST])
    BM_N = din("BM_N", [NST, NST])
    BM_T = din("BM_T", [NST, NST])
    B_IN = din("B_IN", [128, 2])
    B_GATE = din("B_GATE", [128, 8])
    B_OUT = din("B_OUT", [1, D])
    E_BIAS = din("E_BIAS", [1, NEXP])
    GWT = din("GWT", [D, NEXP])
    UPT = din("UPT", [NEXP, D, MOE_I])
    DOWNT = din("DOWNT", [NEXP, MOE_I, D])
    SHUPT = din("SHUPT", [D, SH_I])
    SHDOWNT = din("SHDOWNT", [SH_I, D])
    IDENT = din("IDENT", [128, 128])

    OUT = nc.dram_tensor("OUT", [T_LOC, D], F32, kind="ExternalOutput").ap()
    CSCR = nc.dram_tensor("CSCR", [NEXP, T_LOC], BF16, kind="Internal").ap()

    dbg_taps = {}

    def dbg_tap(name, shape, dtype=F32):
        ap = nc.dram_tensor("DBG_" + name, list(shape), dtype,
                            kind="ExternalOutput").ap()
        dbg_taps[name] = ap
        return ap

    p = Prog()
    r256 = lambda t: t.rearrange("(k p) n -> p k n", p=128)

    al = SbufAlloc(nc, base=16640, limit=229248)
    x_sb = al.take("x", [128, 16, D], F32)
    ident_sb = al.take("ident", [128, 128], BF16)
    bout_sb = al.take("boutb", [128, D], F32)
    ebias_sb = al.take("ebias", [128, NEXP], F32)
    flag_sb = al.take("flag", [128, 1], F32)
    gwt_sb = al.take("gwt", [128, 8, NEXP], BF16)
    bin_sb = al.take("binpc", [128, 2], F32)
    bgate_sb = al.take("bgatepc", [128, 8], F32)
    ct_sb = al.take("cT", [32, 16, 128], BF16)
    stats_sb = al.take("stats", [128, 2, 6], F32)
    mv_sb = al.take("mv", [128, 16, 2], F32)
    rstd_sb = al.take("rstd", [128, 16], F32)
    nmr_sb = al.take("nmr", [128, 16], F32)
    r1_sb = al.take("r1", [128, 1], F32)
    eps_sb = al.take("eps", [128, 1], F32)
    mv2_sb = al.take("mv2", [128, 16, 2, 2], F32)
    tq0_sb = al.take("tq0", [128, 16], F32)
    tq1_sb = al.take("tq1", [128, 16], F32)
    r2_sb = al.take("r2", [128, 1], F32)
    sel_all_sb = al.take("sel_all", [128, 16, NEXP], F32)
    rden_sb = al.take("rden", [128, 16], F32)
    tmp8_sb = al.take("tmp8", [128, 16, 8], F32)
    gs_sb = al.take("gs", [128, 16, 8], F32)
    st_sb = al.take("st", [128, 2, 1], BF16)
    pprev_sb = al.take("pprev", [128, 2, 1], F32)
    k3_sb = al.take("k3", [128, 2, L2], BF16)
    h2_sb = al.take("h2", [128, 2, L2, NB], F32)
    carry_sb = al.take("carry", [128, 2, L2, NB], BF16)

    ps_h = nc.alloc_psum_tensor("PS", [128, 4096], F32)
    PS = ps_h.ap() if hasattr(ps_h, "ap") else ps_h

    from contextlib import ExitStack
    _sem_names = ["t", "v", "s", "p", "c0", "small", "x", "xp", "xp2",
                  "wattn", "w0", "w1", "cb0", "cb1", "cf0", "cf1", "cscr",
                  "c2", "sh", "sh2", "sh3", "out"]
    with ExitStack() as _sctx:
        sems = {n: _sctx.enter_context(nc.semaphore("sem_" + n))
                for n in _sem_names}
        block = _sctx.enter_context(nc.Block())

        x_view = X.rearrange("(a p) h -> p a h", p=128)
        tok_x_q = []
        for q in range(4):
            tok_x_q.append(p.dma("sync", lambda e, q=q: e.dma_start(
                out=x_sb[:, 4 * q:4 * (q + 1), :],
                in_=x_view[:, 4 * q:4 * (q + 1), :]), sem="x"))
        tok_x = tok_x_q[-1]
        p.dma("gpsimd", lambda e: e.dma_start(out=ident_sb, in_=IDENT),
              sem="c0")
        p.dma("sync", lambda e: e.dma_start(
            out=bout_sb, in_=B_OUT.to_broadcast([128, D])), sem="small")
        p.dma("sync", lambda e: e.dma_start(
            out=ebias_sb, in_=E_BIAS.to_broadcast([128, NEXP])), sem="small")
        p.dma("sync", lambda e: e.dma_start(
            out=flag_sb, in_=FLAG.to_broadcast([128, 1])), sem="small")
        tok_gwt = p.dma("gpsimd", lambda e: e.dma_start(
            out=gwt_sb, in_=GWT.rearrange("(k p) n -> p k n", p=128)), sem="c0")
        p.dma("sync", lambda e: e.dma_start(out=bin_sb, in_=B_IN), sem="small")
        tok_small = p.dma("sync", lambda e: e.dma_start(
            out=bgate_sb, in_=B_GATE), sem="small")
        tok_bout = tok_ebias = tok_flag = tok_bin = tok_bgate = tok_small

        a_nat = al.take("a_nat", [128, 2, NST], BF16, side="top")
        a_t = al.take("a_t", [128, 2, NST], BF16, side="top")
        bm_nat = al.take("bm_nat", [128, 2, NST], BF16, side="top")
        bm_t = al.take("bm_t", [128, 2, NST], BF16, side="top")
        alt_sb = al.take("alt", [128, 2, NST], BF16, side="top")
        apow = [None, a_nat] + [
            al.take(f"apow{k}", [128, 2, NST], BF16, side="top") for k in range(2, L1 + 1)]
        gmat = [bm_nat] + [
            al.take(f"g{k}", [128, 2, NST], BF16, side="top") for k in range(1, L1)]
        alpow = [None, None] + [
            al.take(f"alpow{k}", [128, 2, NST], BF16, side="top") for k in range(2, L2 + 1)]
        win_sb = al.take("win", [128, 8, NST], BF16)
        wg_sb = al.take("wg", [128, 8, D], BF16)

        p.dma("gpsimd", lambda e: e.dma_start(out=a_nat, in_=r256(A_N)),
              sem="c0")
        p.dma("gpsimd", lambda e: e.dma_start(out=a_t, in_=r256(A_T)), sem="c0")
        p.dma("gpsimd", lambda e: e.dma_start(out=bm_nat, in_=r256(BM_N)),
              sem="c0")
        tok_c0 = p.dma("gpsimd", lambda e: e.dma_start(
            out=bm_t, in_=r256(BM_T)), sem="c0")
        tok_a = tok_at = tok_bm = tok_bmt = tok_ident = tok_gwt = tok_c0
        tok_win = p.dma("gpsimd", lambda e: e.dma_start(
            out=win_sb, in_=r256(W_IN)), sem="wattn")
        tok_wg0 = p.dma("gpsimd", lambda e: e.dma_start(
            out=wg_sb, in_=r256(W_GATE)), sem="wattn")
        tok_win = tok_wg0

        tok_eps = p.op("vector", lambda e: e.memset(eps_sb, 1e-5), [])

        PS_mat = PS[:, 3072:3584].rearrange("p (m n) -> p m n", m=2)

        def mat_mm(dst_sb, lhsT_sb, rhs_sb, deps):
            last = None
            for m in range(2):
                for nt in range(2):
                    last = p.op("tensor", lambda e, m=m, nt=nt: e.matmul(
                        PS_mat[:, m, :],
                        lhsT_sb[:, nt, m * 128:(m + 1) * 128],
                        rhs_sb[:, nt, :],
                        start=(nt == 0), stop=(nt == 1)), deps)
            return p.op("scalar", lambda e: e.activation(
                out=dst_sb.rearrange("p m n -> p (m n)"),
                in_=PS_mat.rearrange("p m n -> p (m n)"),
                func=AF.Copy, bias=0.0, scale=1.0), [last], fence=False)

        dep0 = [tok_a, tok_at, tok_bm, tok_bmt]
        tok_pow = [None, tok_a]
        prev = tok_a
        for k in range(2, L1 + 1):
            prev = mat_mm(apow[k], a_t, apow[k - 1], dep0 + [prev])
            tok_pow.append(prev)
        tok_g = [tok_bm]
        tg = prev
        for k in range(1, L1):
            tg = mat_mm(gmat[k], bm_t, apow[k], dep0 + [tok_pow[k], tg])
            tok_g.append(tg)
        PS_tr = PS[:, 3584:4096].rearrange("p (m n) -> p m n", m=2)
        last = None
        for nt in range(2):
            for m in range(2):
                last = p.op("tensor", lambda e, nt=nt, m=m: e.matmul(
                    PS_tr[:, m, nt * 128:(nt + 1) * 128],
                    apow[L1][:, nt, m * 128:(m + 1) * 128],
                    ident_sb, start=True, stop=True),
                    [tok_pow[L1], tok_ident])
        tok_alt = p.op("scalar", lambda e: e.activation(
            out=alt_sb.rearrange("p m n -> p (m n)"),
            in_=PS_tr.rearrange("p m n -> p (m n)"),
            func=AF.Copy, bias=0.0, scale=1.0), [last], fence=False)
        alpow[1] = apow[L1]
        tok_alpow = [None, tok_pow[L1]]
        prev = tok_alt
        for k in range(2, L2 + 1):
            prev = mat_mm(alpow[k], alt_sb, alpow[k - 1],
                          [tok_alt, prev, tok_alpow[k - 1]])
            tok_alpow.append(prev)
        xpq = [al.take(f"xpq{q}", [128, 8, D], BF16, side="top")
               for q in range(2)]
        al.free("a_t", "bm_t", "alt")

        ln_chain = [None]

        def emit_ln_stats(src_sb, tt, dep, prev_ln=None, fence=True):
            p.op("vector", lambda e, tt=tt: e.bn_stats(
                out=stats_sb[:, 0, :], in_=src_sb[:, tt, 0:512]),
                [dep, prev_ln], fence=fence)
            p.op("vector", lambda e, tt=tt: e.bn_stats(
                out=stats_sb[:, 1, :], in_=src_sb[:, tt, 512:1024]),
                [dep, prev_ln], fence=False)
            p.op("vector", lambda e, tt=tt: e.bn_aggr(
                out=mv2_sb[:, tt, 0, :], in_=stats_sb[:, 0, :]), [])
            p.op("vector", lambda e, tt=tt: e.bn_aggr(
                out=mv2_sb[:, tt, 1, :], in_=stats_sb[:, 1, :]), [])

        def emit_ln(src_sb, dst_sb, ntt, dep_in, extra_war=(), per_tile_dep=None,
                    norm_eng="scalar", stats_done=False):
            prev_ln = ln_chain[0]
            if not stats_done:
                for tt in range(ntt):
                    dtt = per_tile_dep[tt] if per_tile_dep else dep_in
                    emit_ln_stats(src_sb, tt, dtt, prev_ln)
            m0 = mv2_sb[:, 0:ntt, 0, 0]
            v0 = mv2_sb[:, 0:ntt, 0, 1]
            m1 = mv2_sb[:, 0:ntt, 1, 0]
            v1 = mv2_sb[:, 0:ntt, 1, 1]
            p.op("vector", lambda e: e.tensor_tensor(
                out=tq0_sb[:, 0:ntt], in0=m0, in1=m1, op=ALU.add), [])
            p.op("vector", lambda e: e.tensor_single_scalar(
                out=mv_sb[:, 0:ntt, 0], in_=tq0_sb[:, 0:ntt], scalar=0.5,
                op=ALU.mult), [])
            p.op("vector", lambda e: e.tensor_tensor(
                out=tq0_sb[:, 0:ntt], in0=m0, in1=m1, op=ALU.subtract), [])
            p.op("vector", lambda e: e.tensor_single_scalar(
                out=tq0_sb[:, 0:ntt], in_=tq0_sb[:, 0:ntt], scalar=0.5,
                op=ALU.mult), [])
            p.op("vector", lambda e: e.tensor_tensor(
                out=tq0_sb[:, 0:ntt], in0=tq0_sb[:, 0:ntt],
                in1=tq0_sb[:, 0:ntt], op=ALU.mult), [])
            p.op("vector", lambda e: e.tensor_tensor(
                out=tq1_sb[:, 0:ntt], in0=v0, in1=v1, op=ALU.add), [])
            p.op("vector", lambda e: e.tensor_single_scalar(
                out=tq1_sb[:, 0:ntt], in_=tq1_sb[:, 0:ntt], scalar=0.5,
                op=ALU.mult), [])
            t2 = p.op("vector", lambda e: e.tensor_tensor(
                out=mv_sb[:, 0:ntt, 1], in0=tq1_sb[:, 0:ntt],
                in1=tq0_sb[:, 0:ntt], op=ALU.add), [])
            t3 = p.op("scalar", lambda e: e.activation(
                out=rstd_sb[:, :ntt], in_=mv_sb[:, :ntt, 1],
                func=AF.Ln, bias=eps_sb, scale=1.0), [t2, tok_eps])
            t4 = p.op("scalar", lambda e: e.activation(
                out=rstd_sb[:, :ntt], in_=rstd_sb[:, :ntt],
                func=AF.Exp, bias=0.0, scale=-0.5), [t3])
            t5 = p.op("vector", lambda e: e.scalar_tensor_tensor(
                out=nmr_sb[:, :ntt], in0=mv_sb[:, :ntt, 0], scalar=-1.0,
                in1=rstd_sb[:, :ntt], op0=ALU.mult, op1=ALU.mult), [t4, t2])
            toks = []
            first_s = True
            for tt in range(ntt):
                dtt = per_tile_dep[tt] if per_tile_dep else dep_in
                eng_v = (norm_eng == "vector" or
                         (norm_eng == "mixed" and tt < ntt - 4))
                if eng_v:
                    toks.append(p.op("vector", lambda e, tt=tt: e.tensor_scalar(
                        out=dst_sb[:, tt, :], in0=src_sb[:, tt, :],
                        scalar1=rstd_sb[:, tt:tt + 1],
                        scalar2=nmr_sb[:, tt:tt + 1],
                        op0=ALU.mult, op1=ALU.add),
                        [t5, dtt] + list(extra_war), fence=(tt == 0)))
                else:
                    toks.append(p.op("scalar", lambda e, tt=tt: e.activation(
                        out=dst_sb[:, tt, :], in_=src_sb[:, tt, :],
                        func=AF.Identity, bias=nmr_sb[:, tt:tt + 1],
                        scale=rstd_sb[:, tt:tt + 1]),
                        [t5, dtt] + list(extra_war),
                        fence=(tt == 0 or (norm_eng == "mixed" and first_s))))
                    if norm_eng == "mixed":
                        first_s = False
            ln_chain[0] = toks[-1]
            return toks

        tr_evacs = []

        def emit_transpose(src_sb, dst_sb, ntt, dep_toks, extra_war=()):
            evacs = tr_evacs
            step = len(tr_evacs)
            for grp in range(0, ntt, 4):
                g = min(4, ntt - grp)
                for hk in range(8):
                    par = step % 2
                    ps = PS[:, par * 512: par * 512 + 512]
                    dep = [tok_ident] + list(extra_war)
                    dep.append(dep_toks[min(grp + g - 1, len(dep_toks) - 1)])
                    if len(evacs) >= 2:
                        dep.append(evacs[-2])
                    last = None
                    for j in range(g):
                        tt = grp + j
                        last = p.op("tensor",
                            lambda e, j=j, tt=tt, hk=hk, ps=ps: e.matmul(
                                ps[:, j * 128:(j + 1) * 128],
                                src_sb[:, tt, hk * 128:(hk + 1) * 128],
                                ident_sb, start=True, stop=True), dep)
                    evacs.append(p.op("scalar",
                        lambda e, hk=hk, grp=grp, g=g, ps=ps: e.activation(
                            out=dst_sb[:, hk, grp * 128:(grp + g) * 128],
                            in_=ps[:, : g * 128],
                            func=AF.Copy, bias=0.0, scale=1.0), [last],
                        fence=False))
                    step += 1
            return evacs

        def emit_lvl23(p_src, dep_p, init_tok, want_carry):
            PS_h2 = PS[:, 2048:2560].rearrange("p (m n) -> p m n", m=2)
            last = None
            for m in range(2):
                first = True
                for k in range(L2):
                    for nt in range(2):
                        rhs = p_src[:, nt, 0:L2 - k, :].rearrange(
                            "p a b -> p (a b)")
                        out = PS_h2[:, m, k * NB:]
                        if k == 0:
                            if nt != m:
                                continue
                            last = p.op("tensor",
                                lambda e, rhs=rhs, out=out, first=first: e.matmul(
                                    out, ident_sb, rhs,
                                    start=first, stop=False),
                                [dep_p, tok_ident])
                        else:
                            last = p.op("tensor",
                                lambda e, k=k, nt=nt, m=m, rhs=rhs, out=out,
                                first=first: e.matmul(
                                    out, alpow[k][:, nt, m * 128:(m + 1) * 128],
                                    rhs, start=first,
                                    stop=(k == L2 - 1 and nt == 1)),
                                [dep_p, tok_alpow[k]])
                        first = False
            tok_h2 = p.op("scalar", lambda e: e.activation(
                out=h2_sb.rearrange("p m a b -> p (m a b)"),
                in_=PS_h2.rearrange("p m n -> p (m n)"),
                func=AF.Copy, bias=0.0, scale=1.0), [last], fence=False)
            PS_st = PS[:, 3072:3074].rearrange("p (m n) -> p m n", m=2)
            tok_prev = [tok_h2, init_tok]
            k3_last = None
            for b in range(NB):
                if want_carry:
                    k3_last = p.op("vector", lambda e, b=b: e.tensor_copy(
                        out=k3_sb[:, :, b:b + 1], in_=st_sb), tok_prev)
                last = None
                for m in range(2):
                    for nt in range(2):
                        last = p.op("tensor", lambda e, m=m, nt=nt: e.matmul(
                            PS_st[:, m, :],
                            alpow[L2][:, nt, m * 128:(m + 1) * 128],
                            st_sb[:, nt, :],
                            start=(nt == 0), stop=(nt == 1)),
                            tok_prev + [tok_alpow[L2]])
                tok_st = p.op("vector", lambda e, b=b: e.tensor_tensor(
                    out=st_sb.rearrange("p m n -> p (m n)"),
                    in0=PS_st.rearrange("p m n -> p (m n)"),
                    in1=h2_sb[:, :, L2 - 1, b],
                    op=ALU.add), [last, tok_h2])
                tok_prev = [tok_st]
            if not want_carry:
                return tok_prev[0]
            PS_c2 = PS[:, 2560:3072].rearrange("p (m o b) -> p m o b",
                                               m=2, o=L2)
            last = None
            for o in range(L2):
                for m in range(2):
                    for nt in range(2):
                        last = p.op("tensor",
                            lambda e, o=o, m=m, nt=nt: e.matmul(
                                PS_c2[:, m, o, :],
                                alpow[o + 1][:, nt, m * 128:(m + 1) * 128],
                                k3_sb[:, nt, :],
                                start=(nt == 0), stop=(nt == 1)),
                            [k3_last, tok_alpow[o + 1], tok_prev[0]])
            tok_c1 = p.op("vector", lambda e: e.tensor_tensor(
                out=carry_sb[:, :, 1:L2, :], in0=PS_c2[:, :, 0:L2 - 1, :],
                in1=h2_sb[:, :, 0:L2 - 1, :], op=ALU.add),
                [last, tok_h2], fence=False)
            tok_c2 = p.op("vector", lambda e: e.tensor_tensor(
                out=carry_sb[:, :, 0:1, 1:NB],
                in0=PS_c2[:, :, L2 - 1:L2, 0:NB - 1],
                in1=h2_sb[:, :, L2 - 1:L2, 0:NB - 1], op=ALU.add),
                [last], fence=False)
            tok_c0 = p.op("vector", lambda e: e.tensor_copy(
                out=carry_sb[:, :, 0, 0:1], in_=pprev_sb.rearrange(
                    "p m n -> p m n")), [tok_c2, init_tok])
            return tok_c0

        u_prev = al.take("u_prev", [128, 2, T_LOC], BF16, side="top")
        xnpT_sb = al.take("xnpT_h", [128, 8, T_LOC], BF16, side="top")

        tok_u_prev = []
        xp_view = XPREV.rearrange("(a p) h -> p a h", p=128)
        tok_xp_h = []
        for q in range(2):
            tok_xp_h.append(p.dma("gpsimd", lambda e, q=q: e.dma_start(
                out=xpq[q], in_=xp_view[:, 8 * q:8 * (q + 1), :]),
                sem="xp" if q == 0 else "xp2"))
        for q in range(2):
            ln_toks = emit_ln(xpq[q], xpq[q], 8, tok_xp_h[q],
                              norm_eng="vector")
            n0 = len(tr_evacs)
            tr_toks = emit_transpose(
                xpq[q], xnpT_sb[:, :, q * 1024:(q + 1) * 1024], 8, ln_toks,
                extra_war=([tok_u_prev[-1]] if tok_u_prev else []))
            trl = tr_toks[n0:]
            for tcl in range(2):
                tc = q * 2 + tcl
                for nt in range(2):
                    par = (tc * 2 + nt) % 2
                    ps = PS[:, 1024 + par * 512: 1536 + par * 512]
                    last = None
                    dep = [trl[tcl * 8 + 7], tok_win]
                    if len(tok_u_prev) >= 2:
                        dep.append(tok_u_prev[-2])
                    for hk in range(8):
                        last = p.op("tensor",
                            lambda e, nt=nt, tc=tc, hk=hk, ps=ps: e.matmul(
                                ps, win_sb[:, hk, nt * 128:(nt + 1) * 128],
                                xnpT_sb[:, hk, tc * 512:(tc + 1) * 512],
                                start=(hk == 0), stop=(hk == 7)), dep)
                    tok_u_prev.append(p.op("scalar",
                        lambda e, nt=nt, tc=tc, ps=ps: e.activation(
                            out=u_prev[:, nt, tc * 512:(tc + 1) * 512],
                            in_=ps, func=AF.Identity,
                            bias=bin_sb[:, nt:nt + 1], scale=1.0),
                        [last, tok_bin], fence=False))
        PS_pp = PS[:, 2048:2560].rearrange("p (m n) -> p m n", m=2)
        last = None
        for m in range(2):
            first = True
            for l in range(L1):
                for nt in range(2):
                    last = p.op("tensor",
                        lambda e, m=m, l=l, nt=nt, first=first: e.matmul(
                            PS_pp[:, m, :],
                            gmat[L1 - 1 - l][:, nt, m * 128:(m + 1) * 128],
                            u_prev[:, nt, :].rearrange(
                                "p (c l) -> p c l", l=L1)[:, :, l],
                            start=first, stop=(l == L1 - 1 and nt == 1)),
                        [tok_u_prev[-1], tok_g[L1 - 1 - l]])
                    first = False
        pp_sb = al.take("pp", [128, 2, L2, NB], BF16, side="top")
        tok_pp = p.op("scalar", lambda e: e.activation(
            out=pp_sb.rearrange("p m c b -> p m b c"),
            in_=PS_pp,
            func=AF.Copy, bias=0.0, scale=1.0), [last], fence=False)

        al.free("xpq0", "xpq1", "xnpT_h")

        xn_sb = al.take("xn", [128, 16, D], BF16, side="top")
        xnT_sb = al.take("xnT", [128, 8, T_LOC], BF16)
        ln1_toks = emit_ln(x_sb, xn_sb, 16, tok_x, norm_eng="mixed",
                           extra_war=[tok_u_prev[-1]])
        tr1_toks = emit_transpose(xn_sb, xnT_sb, 16, ln1_toks)
        tok_xnT = tr1_toks[-1]
        if debug:
            d = dbg_tap("xnT", [128, 8 * T_LOC], BF16)
            p.dma("sync", lambda e, d=d: e.dma_start(
                out=d, in_=xnT_sb.rearrange("p a b -> p (a b)")), [tok_xnT], sem="out")

        al.free("xn")

        u_nat = al.take("u_nat", [128, 2, T_LOC], BF16)
        u_int = al.take("u_int", [128, 2, L1, NCH], BF16, side="top")
        tok_unat = []
        for nt in range(2):
            for tc in range(4):
                par = (nt * 4 + tc) % 2
                ps = PS[:, par * 512:(par + 1) * 512]
                dep = [tok_xnT, tok_win]
                if len(tok_unat) >= 2:
                    dep.append(tok_unat[-2])
                last = None
                for hk in range(8):
                    last = p.op("tensor",
                        lambda e, nt=nt, tc=tc, hk=hk, ps=ps: e.matmul(
                            ps, win_sb[:, hk, nt * 128:(nt + 1) * 128],
                            xnT_sb[:, hk, tc * 512:(tc + 1) * 512],
                            start=(hk == 0), stop=(hk == 7)), dep)
                tok_unat.append(p.op("scalar",
                    lambda e, nt=nt, tc=tc, ps=ps: e.activation(
                        out=u_nat[:, nt, tc * 512:(tc + 1) * 512], in_=ps,
                        func=AF.Identity, bias=bin_sb[:, nt:nt + 1], scale=1.0),
                    [last, tok_bin], fence=False))
        tok_uint = []
        for nt in range(2):
            tok_uint.append(p.op("vector", lambda e, nt=nt: e.tensor_copy(
                out=u_int[:, nt, :, :].rearrange("p l c -> p c l"),
                in_=u_nat[:, nt, :]), [tok_unat[-1]], fence=False))
        if debug:
            d = dbg_tap("u_nat", [128, 2 * T_LOC], BF16)
            p.dma("sync", lambda e, d=d: e.dma_start(
                out=d, in_=u_nat.rearrange("p a b -> p (a b)")), [tok_unat[-1]], sem="out")

        tok_z = p.op("vector", lambda e: e.memset(
            st_sb.rearrange("p m n -> p (m n)"), 0.0), [])
        tok_prev_final = emit_lvl23(pp_sb, tok_pp, tok_z, want_carry=False)
        tok_pprev = p.op("vector", lambda e: e.scalar_tensor_tensor(
            out=pprev_sb.rearrange("p m n -> p (m n)"),
            in0=st_sb.rearrange("p m n -> p (m n)"),
            scalar=1.0, in1=flag_sb.to_broadcast([128, 2]),
            op0=ALU.mult, op1=ALU.mult), [tok_prev_final, tok_flag])
        if debug:
            d = dbg_tap("pprev", [128, 2])
            p.dma("sync", lambda e, d=d: e.dma_start(
                out=d, in_=pprev_sb.rearrange("p m n -> p (m n)")),
                [tok_pprev], sem="out")
        al.free("pp")

        S_int = al.take("S_int", [128, 2, L1, NCH], BF16, side="top")
        hs_sb = al.take("hs", [128, 2, T_LOC], BF16)
        tok_Sevac = []
        for tb in range(4):
            par = tb % 2
            ps = PS[:, par * 1024:(par + 1) * 1024].rearrange(
                "p (m l c) -> p m l c", m=2, l=L1)
            last = None
            for m in range(2):
                first = True
                for k in range(L1):
                    for nt in range(2):
                        dep = [tok_uint[nt], tok_g[k]]
                        if len(tok_Sevac) >= 2:
                            dep.append(tok_Sevac[-2])
                        last = p.op("tensor",
                            lambda e, m=m, k=k, nt=nt, tb=tb, ps=ps,
                            first=first: e.matmul(
                                ps[:, m, k:L1, :].rearrange("p a b -> p (a b)"),
                                gmat[k][:, nt, m * 128:(m + 1) * 128],
                                u_int[:, nt, 0:L1 - k, tb * 64:(tb + 1) * 64],
                                start=first,
                                stop=(k == L1 - 1 and nt == 1)), dep)
                        first = False
            tok_Sevac.append(p.op("scalar", lambda e, tb=tb, ps=ps: e.activation(
                out=S_int[:, :, :, tb * 64:(tb + 1) * 64],
                in_=ps, func=AF.Copy, bias=0.0, scale=1.0), [last],
                fence=False))
        tok_S = tok_Sevac[-1]
        tok_init = p.op("vector", lambda e: e.tensor_copy(
            out=st_sb, in_=pprev_sb), [tok_pprev])
        pf_sb = al.take("pf", [128, 2, L2, NB], BF16, side="top")
        tok_pf = p.op("vector", lambda e: e.tensor_copy(
            out=pf_sb,
            in_=S_int[:, :, L1 - 1, :].rearrange(
                "p m (b c) -> p m c b", c=L2)), [tok_S], fence=False)
        tok_carry = emit_lvl23(pf_sb, tok_pf, tok_init, want_carry=True)
        tok_hs = []
        corr_mm_last = []
        for o in range(L1):
            base = o * 512
            PS_c = PS[:, base:base + 512].rearrange("p (m n) -> p m n", m=2)
            last = None
            for m in range(2):
                for nt in range(2):
                    dep = [tok_carry, tok_pow[o + 1]]
                    last = p.op("tensor",
                        lambda e, o=o, m=m, nt=nt, PS_c=PS_c: e.matmul(
                            PS_c[:, m, :],
                            apow[o + 1][:, nt, m * 128:(m + 1) * 128],
                            carry_sb[:, nt, :, :].rearrange("p a b -> p (a b)"),
                            start=(nt == 0), stop=False), dep)
                last = p.op("tensor",
                    lambda e, o=o, m=m, PS_c=PS_c: e.matmul(
                        PS_c[:, m, :], ident_sb,
                        S_int[:, m, o, :].rearrange("p (b c) -> p c b", c=L2),
                        start=False, stop=True), [tok_S, tok_ident])
            corr_mm_last.append(last)
        for o in range(L1):
            base = o * 512
            PS_c = PS[:, base:base + 512].rearrange("p (m n) -> p m n", m=2)
            tok_hs.append(p.op("vector", lambda e, o=o, PS_c=PS_c: e.tensor_copy(
                out=hs_sb.rearrange(
                    "p m (b c l) -> p m c b l", c=L2, l=L1)[:, :, :, :, o],
                in_=PS_c.rearrange("p m (c b) -> p m c b", b=NB)),
                [corr_mm_last[o]], fence=False))
        tok_hs_done = tok_hs[-1]
        if debug:
            d = dbg_tap("hs", [128, 2 * T_LOC], BF16)
            p.dma("sync", lambda e, d=d: e.dma_start(
                out=d, in_=hs_sb.rearrange("p a b -> p (a b)")), [tok_hs_done], sem="out")
        tok_xb = []
        for tt in range(16):
            tok_xb.append(p.op("vector", lambda e, tt=tt: e.tensor_tensor(
                out=x_sb[:, tt, :], in0=x_sb[:, tt, :], in1=bout_sb,
                op=ALU.add), [ln1_toks[tt], tok_bout], fence=False))

        for k in range(2, L1 + 1):
            al.free(f"apow{k}")
        for k in range(1, L1):
            al.free(f"g{k}")
        for k in range(2, L2 + 1):
            al.free(f"alpow{k}")
        al.free("a_nat", "bm_nat", "pf")
        gate_sb = al.take("gate", [128, 8, T_LOC], BF16, side="top")
        al.free("S_int", "u_int", "u_prev", "win")
        al.free("h2", "carry", "k3", "st", "pprev")

        wo_sb = al.take("wo", [128, 8, D], BF16)
        ctw_sb = al.take("ctw", [128, 2, D], BF16)
        dtw_sb = al.take("dtw", [128, 2, D], BF16)
        wdep = [tok_hs_done, corr_mm_last[-1]]
        p.dma("gpsimd", lambda e: e.dma_start(out=wo_sb, in_=r256(W_OUT)),
              wdep, sem="wattn")
        p.dma("gpsimd", lambda e: e.dma_start(out=ctw_sb, in_=r256(CT)),
              wdep, sem="wattn")
        tok_wattn2 = p.dma("gpsimd", lambda e: e.dma_start(
            out=dtw_sb, in_=r256(DT)), wdep, sem="wattn")
        tok_wo = tok_ct = tok_dt = tok_wattn2
        tok_wg = tok_win
        tok_gate = []
        tok_gate_dmtc = {}
        for dm in range(8):
            for tc in range(4):
                par = (dm * 4 + tc) % 2
                ps = PS[:, par * 512:(par + 1) * 512]
                dep = [tok_xnT, tok_wg, tok_S]
                if len(tok_gate) >= 2:
                    dep.append(tok_gate[-2])
                else:
                    dep.append(tok_hs[par])
                last = None
                for hk in range(8):
                    last = p.op("tensor",
                        lambda e, dm=dm, tc=tc, hk=hk, ps=ps: e.matmul(
                            ps, wg_sb[:, hk, dm * 128:(dm + 1) * 128],
                            xnT_sb[:, hk, tc * 512:(tc + 1) * 512],
                            start=(hk == 0), stop=(hk == 7)), dep)
                sdep = [last, tok_bgate]
                if len(tok_gate) < 2:
                    sdep.append(corr_mm_last[-1])
                tok_gate.append(p.op("scalar",
                    lambda e, dm=dm, tc=tc, ps=ps: e.activation(
                        out=gate_sb[:, dm, tc * 512:(tc + 1) * 512], in_=ps,
                        func=AF.Sigmoid, bias=bgate_sb[:, dm:dm + 1], scale=1.0),
                    sdep, fence=False))
                tok_gate_dmtc[(dm, tc)] = tok_gate[-1]
        if debug:
            d = dbg_tap("gate", [128, 8 * T_LOC], BF16)
            p.dma("sync", lambda e, d=d: e.dma_start(
                out=d, in_=gate_sb.rearrange("p a b -> p (a b)")), [tok_gate[-1]], sem="out")

        p.barrier()
        al.free("xnT", "wg")

        yT_sb = al.take("yT", [128, 8, T_LOC], BF16, side="top")
        tok_y = []
        tok_y_tc = {}
        for tc in range(4):
            for dm in range(8):
                par = (tc * 8 + dm) % 2
                ps = PS[:, 1024 + par * 512:1536 + par * 512]
                dep = [tok_hs_done, tok_ct, tok_unat[-1], tok_dt]
                if len(tok_y) >= 2:
                    dep.append(tok_y[-2])
                last = None
                for nt in range(2):
                    last = p.op("tensor",
                        lambda e, dm=dm, tc=tc, nt=nt, ps=ps: e.matmul(
                            ps, ctw_sb[:, nt, dm * 128:(dm + 1) * 128],
                            hs_sb[:, nt, tc * 512:(tc + 1) * 512],
                            start=(nt == 0), stop=False), dep)
                for nt in range(2):
                    last = p.op("tensor",
                        lambda e, dm=dm, tc=tc, nt=nt, ps=ps: e.matmul(
                            ps, dtw_sb[:, nt, dm * 128:(dm + 1) * 128],
                            u_nat[:, nt, tc * 512:(tc + 1) * 512],
                            start=False, stop=(nt == 1)), [tok_dt])
                tok_y.append(p.op("vector",
                    lambda e, dm=dm, tc=tc, ps=ps: e.tensor_tensor(
                        out=yT_sb[:, dm, tc * 512:(tc + 1) * 512],
                        in0=ps, in1=gate_sb[:, dm, tc * 512:(tc + 1) * 512],
                        op=ALU.mult), [last, tok_gate_dmtc[(dm, tc)]],
                    fence=False))
            tok_y_tc[tc] = tok_y[-1]

        al.free("gate", "u_nat", "hs")
        ao_sb = al.take("ao", [128, 16, D], BF16, side="top")
        tok_ao = []
        tok_ao_tt = {}
        for tt in range(16):
            for hc in range(2):
                par = (tt * 2 + hc) % 8
                ps = PS[:, par * 512:(par + 1) * 512]
                dep = [tok_y_tc[tt // 4], tok_wo]
                if par in (2, 3):
                    dep.append(tok_y[-1])
                if len(tok_ao) >= 8:
                    dep.append(tok_ao[-8])
                last = None
                for dm in range(8):
                    last = p.op("tensor",
                        lambda e, tt=tt, hc=hc, dm=dm, ps=ps: e.matmul(
                            ps, yT_sb[:, dm, tt * 128:(tt + 1) * 128],
                            wo_sb[:, dm, hc * 512:(hc + 1) * 512],
                            start=(dm == 0), stop=(dm == 7)), dep)
                tok_ao.append(p.op("scalar",
                    lambda e, tt=tt, hc=hc, ps=ps: e.activation(
                        out=ao_sb[:, tt, hc * 512:(hc + 1) * 512],
                        in_=ps, func=AF.Copy, bias=0.0, scale=1.0),
                    [last, tok_y[-1]], fence=False))
        ln2_prev = ln_chain[0]
        for tt in range(16):
            tok_ao_tt[tt] = p.op("vector", lambda e, tt=tt: e.tensor_tensor(
                out=x_sb[:, tt, :], in0=x_sb[:, tt, :], in1=ao_sb[:, tt, :],
                op=ALU.add), [tok_ao[2 * tt + 1], tok_xb[-1]],
                fence=(tt == 0))
            emit_ln_stats(x_sb, tt, tok_ao_tt[tt],
                          ln2_prev if tt == 0 else None, fence=True)
        tok_ao.append(tok_ao_tt[15])
        if debug:
            d = dbg_tap("x1", [128, 16 * D])
            p.dma("sync", lambda e, d=d: e.dma_start(
                out=d, in_=x_sb.rearrange("p a b -> p (a b)")), [tok_ao[-1]], sem="out")

        al.free("yT", "wo", "ctw", "dtw", "boutb", "ao")

        wup = [al.take("wup0", [128, 8, 2, MOE_I], BF16), None]
        wdn = [al.take("wdn0", [128, 4, 2, D], BF16), None]
        cb = [al.take("cb0", [128, 2, T_LOC], BF16), None]
        hr = [al.take("hr0", [128, 4, 256], BF16),
              al.take("hr1", [128, 4, 256], BF16)]
        hsq = [al.take("hsq0", [128, 4, 256], BF16),
               al.take("hsq1", [128, 4, 256], BF16)]

        w_toks = {}
        w_war = {}
        cb_toks = {}
        cbf_toks = {}
        tok_cscr0_box = [None]
        cb_war = {}
        tok_cscr_box = [None]

        def load_group_weights(g, extra_dep=None):
            par = g % 2
            dep = [w_war.get(g - 2), extra_dep]
            t = None
            for ei in range(2):
                ex = 2 * g + ei
                p.dma("gpsimd", lambda e, ex=ex, ei=ei, par=par: e.dma_start(
                    out=wup[par][:, :, ei, :],
                    in_=UPT[ex, :, :].rearrange("(k p) j -> p k j", p=128)),
                    dep, sem=f"w{par}")
                t = p.dma("gpsimd", lambda e, ex=ex, ei=ei, par=par: e.dma_start(
                    out=wdn[par][:, :, ei, :],
                    in_=DOWNT[ex, :, :].rearrange("(k p) j -> p k j", p=128)),
                    dep, sem=f"w{par}")
            w_toks[g] = t

        def load_group_cb(g):
            par = g % 2
            if g < 2:
                cbf_toks[g] = p.dma("sync", lambda e, g=g, par=par: e.dma_start(
                    out=cb[par][:, :, 0:256],
                    in_=CSCR[2 * g:2 * g + 2, 0:256].unsqueeze(0)
                    .to_broadcast([128, 2, 256])),
                    [tok_cscr0_box[0], cb_war.get(g - 2)], sem=f"cf{par}")
                cb_toks[g] = p.dma("sync", lambda e, g=g, par=par: e.dma_start(
                    out=cb[par][:, :, 256:T_LOC],
                    in_=CSCR[2 * g:2 * g + 2, 256:T_LOC].unsqueeze(0)
                    .to_broadcast([128, 2, T_LOC - 256])),
                    [tok_cscr_box[0], cb_war.get(g - 2)], sem=f"cb{par}")
            else:
                cb_toks[g] = p.dma("sync", lambda e, g=g, par=par: e.dma_start(
                    out=cb[par],
                    in_=CSCR[2 * g:2 * g + 2, :].unsqueeze(0).to_broadcast(
                        [128, 2, T_LOC])),
                    [tok_cscr0_box[0], tok_cscr_box[0], cb_war.get(g - 2)],
                    sem=f"cb{par}")

        load_group_weights(0, extra_dep=tok_ao[-1])

        xn2_sb = al.take("xn2", [128, 16, D], BF16, side="top")
        xn2T_sb = al.take("xn2T", [128, 8, T_LOC], BF16)
        ln2_toks = emit_ln(x_sb, xn2_sb, 16, tok_ao[-1],
                           per_tile_dep=tok_ao_tt, norm_eng="mixed",
                           stats_done=True)
        tr2_toks = emit_transpose(xn2_sb, xn2T_sb, 16, ln2_toks)
        tok_xn2T = tr2_toks[-1]

        p.barrier()
        al.free("xn2")
        wup[1] = al.take("wup1", [128, 8, 2, MOE_I], BF16)
        wdn[1] = al.take("wdn1", [128, 4, 2, D], BF16)
        cb[1] = al.take("cb1", [128, 2, T_LOC], BF16)
        load_group_weights(1)

        scores_sb = al.take("scores", [128, 16, NEXP], F32)
        sc_sb = al.take("sc", [128, 16, NEXP], F32)
        masked_sb = al.take("masked", [128, 16, NEXP], F32)
        gm8_sb = tmp8_sb
        gk_sb = sel_all_sb[:, :, 0:8]
        gmask_sb = tmp8_sb
        max8_sb = tmp8_sb
        comb_sb = al.take("comb", [128, 16, NEXP], F32)
        combb_sb = al.take("combb", [128, 16, NEXP], BF16)
        knock_sb = comb_sb
        lgT_sb = al.take("lgT", [32, 2, 512], F32)
        identf_sb = al.take("identf", [32, 32], F32)
        tok_idf = p.dma("sync", lambda e: e.dma_start(
            out=identf_sb, in_=IDENT[0:32, 0:32]), sem="cscr")
        tok_scores = None
        for half in range(2):
            lg_evacs = {}
            for tc in (2 * half, 2 * half + 1):
                slot = tc % 2
                ps = PS[0:32, 2048 + slot * 512:2560 + slot * 512]
                last = None
                for hk in range(8):
                    last = p.op("tensor",
                        lambda e, tc=tc, hk=hk, ps=ps: e.matmul(
                            ps, gwt_sb[:, hk, :],
                            xn2T_sb[:, hk, tc * 512:(tc + 1) * 512],
                            start=(hk == 0), stop=(hk == 7)),
                        [tok_xn2T, tok_gwt])
                lg_evacs[slot] = p.op("scalar",
                    lambda e, slot=slot, ps=ps: e.activation(
                        out=lgT_sb[:, slot, :], in_=ps,
                        func=AF.Copy, bias=0.0, scale=1.0), [last],
                    fence=False)
            ps2 = PS[:, half * 512: half * 512 + 256].rearrange(
                "p (a b) -> p a b", b=NEXP)
            last = None
            for j in range(8):
                tt = half * 8 + j
                slot = (tt // 4) % 2
                last = p.op("tensor",
                    lambda e, j=j, tt=tt, slot=slot, ps2=ps2: e.matmul(
                        ps2[:, j, :],
                        lgT_sb[:, slot, (tt % 4) * 128:(tt % 4 + 1) * 128],
                        identf_sb, start=True, stop=True),
                    [lg_evacs[slot], tok_idf])
            tok_scores = p.op("scalar", lambda e, half=half, ps2=ps2: e.activation(
                out=scores_sb[:, half * 8:(half + 1) * 8, :].rearrange(
                    "p a b -> p (a b)"),
                in_=ps2.rearrange("p a b -> p (a b)"),
                func=AF.Sigmoid, bias=0.0, scale=1.0), [last], fence=False)
        tok_sc = p.op("vector", lambda e: e.tensor_tensor(
            out=sc_sb, in0=scores_sb,
            in1=ebias_sb.unsqueeze(1).to_broadcast([128, 16, NEXP]),
            op=ALU.add), [tok_scores, tok_ebias])
        g4 = sc_sb.rearrange("p a (g m) -> p a g m", m=4)
        pairs = [(0, 1), (0, 2), (0, 3), (1, 2), (1, 3), (2, 3)]
        tok_gs = p.op("vector", lambda e: e.tensor_tensor(
            out=gs_sb, in0=g4[:, :, :, 0], in1=g4[:, :, :, 1], op=ALU.add),
            [tok_sc])
        for (i, j) in pairs[1:]:
            t_a = p.op("vector", lambda e, i=i, j=j: e.tensor_tensor(
                out=tmp8_sb, in0=g4[:, :, :, i], in1=g4[:, :, :, j],
                op=ALU.add), [tok_sc])
            tok_gs = p.op("vector", lambda e: e.tensor_tensor(
                out=gs_sb, in0=gs_sb, in1=tmp8_sb, op=ALU.max), [t_a])
        for tt in range(16):
            p.op("vector", lambda e, tt=tt: e.max(
                out=gm8_sb[:, tt, :], in_=gs_sb[:, tt, :]), [],
                fence=(tt == 0))
        p.op("vector", lambda e: e.memset(gm8_sb[:, :, 4:8], NEG_SENT), [])
        for tt in range(16):
            p.op("vector", lambda e, tt=tt: e.match_replace(
                out=gk_sb[:, tt, :], in_to_replace=gm8_sb[:, tt, :],
                in_values=gs_sb[:, tt, :], imm_value=KNOCK), [],
                fence=(tt == 0))
        p.op("vector", lambda e: e.tensor_single_scalar(
            out=gmask_sb, in_=gk_sb,
            scalar=KNOCK, op=ALU.is_equal), [])
        p.op("vector", lambda e: e.tensor_tensor(
            out=masked_sb.rearrange("p a (g m) -> p (a g) m", m=4),
            in0=sc_sb.rearrange("p a (g m) -> p (a g) m", m=4),
            in1=gmask_sb.rearrange("p a b -> p (a b)").unsqueeze(2)
                .to_broadcast([128, 128, 4]),
            op=ALU.mult), [])
        for tt in range(16):
            p.op("vector", lambda e, tt=tt: e.max(
                out=max8_sb[:, tt, :], in_=masked_sb[:, tt, :]), [],
                fence=(tt == 0))
        p.op("vector", lambda e: e.memset(max8_sb[:, :, 6:8], NEG_SENT), [])
        for tt in range(16):
            p.op("vector", lambda e, tt=tt: e.match_replace(
                out=knock_sb[:, tt, :], in_to_replace=max8_sb[:, tt, :],
                in_values=masked_sb[:, tt, :], imm_value=KNOCK), [],
                fence=(tt == 0))
        tok_comb = p.op("vector", lambda e: e.tensor_single_scalar(
            out=sel_all_sb.rearrange("p a b -> p (a b)"),
            in_=knock_sb.rearrange("p a b -> p (a b)"),
            scalar=KNOCK, op=ALU.is_equal), [])
        p.op("vector", lambda e: e.tensor_tensor(
            out=comb_sb.rearrange("p a b -> p (a b)"),
            in0=sel_all_sb.rearrange("p a b -> p (a b)"),
            in1=scores_sb.rearrange("p a b -> p (a b)"), op=ALU.mult), [])
        p.op("vector", lambda e: e.tensor_reduce(
            out=rden_sb, in_=comb_sb, axis=AX.X, op=ALU.add), [])
        p.op("vector", lambda e: e.tensor_single_scalar(
            out=rden_sb, in_=rden_sb, scalar=1e-20, op=ALU.add), [])
        p.op("vector", lambda e: e.reciprocal(out=rden_sb, in_=rden_sb), [])
        p.op("vector", lambda e: e.tensor_single_scalar(
            out=rden_sb, in_=rden_sb, scalar=2.5, op=ALU.mult), [])
        tok_comb = p.op("vector", lambda e: e.tensor_tensor(
            out=comb_sb, in0=comb_sb,
            in1=rden_sb.unsqueeze(2).to_broadcast([128, 16, NEXP]),
            op=ALU.mult), [])
        tok_combb = p.op("vector", lambda e: e.tensor_copy(
            out=combb_sb.rearrange("p a b -> p (a b)"),
            in_=comb_sb.rearrange("p a b -> p (a b)")), [tok_comb])

        NG = 16
        ps_up = [PS[:, 0:1024].rearrange("p (m n) -> p m n", m=4),
                 PS[:, 1024:2048].rearrange("p (m n) -> p m n", m=4)]
        ps_acc = PS[:, 2048:4096].rearrange("p (m n) -> p m n", m=2)
        prev_relu = {}
        prev_dn = {}
        acc_evac_last = [None]
        early_sh_up = {}
        shup_sb = None
        tok_shup = None
        pre_tq = {}
        pre_lastup = {}

        def moe_ups(g, tb, ei):
            par = g % 2
            upar = ei
            dep = [w_toks[g], tok_xn2T, prev_relu.get(upar), tok_scores]
            last_up = None
            for mi in range(4):
                for hk in range(8):
                    last_up = p.op("tensor",
                        lambda e, par=par, ei=ei, mi=mi, hk=hk, tb=tb,
                        upar=upar: e.matmul(
                            ps_up[upar][:, mi, :],
                            wup[par][:, hk, ei, mi * 128:(mi + 1) * 128],
                            xn2T_sb[:, hk, tb * 256:(tb + 1) * 256],
                            start=(hk == 0), stop=(hk == 7)), dep)
            return last_up

        def moe_relu_sq(g, tb, ei, last_up):
            upar = ei
            t_r = p.op("scalar", lambda e, upar=upar: e.activation(
                out=hr[upar].rearrange("p m n -> p (m n)"),
                in_=ps_up[upar].rearrange("p m n -> p (m n)"),
                func=AF.Relu, bias=0.0, scale=1.0),
                [last_up, prev_dn.get(upar)], fence=False)
            prev_relu[upar] = t_r
            return p.op("vector", lambda e, upar=upar: e.tensor_tensor(
                out=hsq[upar].rearrange("p m n -> p (m n)"),
                in0=hr[upar].rearrange("p m n -> p (m n)"),
                in1=hr[upar].rearrange("p m n -> p (m n)"),
                op=ALU.mult), [t_r], fence=False)

        for ei in range(2):
            pre_tq[(0, ei)] = moe_relu_sq(0, 0, ei, moe_ups(0, 0, ei))
        for ei in range(2):
            pre_lastup[(1, ei)] = moe_ups(0, 1, ei)

        tok_ctr = []
        for grp in range(4):
            par = grp % 2
            ps = PS[0:32, 2048 + par * 512:2560 + par * 512]
            dep = [tok_combb, tok_ident, tok_ao[-1]]
            if len(tok_ctr) >= 2:
                dep.append(tok_ctr[-2])
            last = None
            for j in range(4):
                tt = grp * 4 + j
                last = p.op("tensor", lambda e, j=j, tt=tt, ps=ps: e.matmul(
                    ps[:, j * 128:(j + 1) * 128],
                    combb_sb[:, tt, :], ident_sb, start=True, stop=True), dep)
            tok_ctr.append(p.op("scalar", lambda e, grp=grp, ps=ps: e.activation(
                out=ct_sb[:, grp * 4:(grp + 1) * 4, :].rearrange(
                    "q a b -> q (a b)"),
                in_=ps, func=AF.Copy, bias=0.0, scale=1.0), [last],
                fence=False))
        tok_cscr0 = p.dma("sync", lambda e: e.dma_start(
            out=CSCR[:, 0:256],
            in_=ct_sb[:, 0:2, :].rearrange("q a b -> q (a b)")),
            [tok_ctr[0]], sem="cscr")
        tok_cscr1 = p.dma("sync", lambda e: e.dma_start(
            out=CSCR[:, 256:T_LOC],
            in_=ct_sb[:, 2:16, :].rearrange("q a b -> q (a b)")),
            [tok_ctr[-1]], sem="c2")
        tok_cscr_box[0] = tok_cscr1
        tok_cscr0_box[0] = tok_cscr0
        if debug:
            d = dbg_tap("combine", [128, 16 * NEXP])
            p.dma("sync", lambda e, d=d: e.dma_start(
                out=d, in_=comb_sb.rearrange("p a b -> p (a b)")), [tok_comb], sem="out")

        load_group_cb(0)
        load_group_cb(1)
        al.free("scores", "sc", "masked", "comb", "combb", "cT", "lgT",
                "identf")

        for g in range(NG):
            par = g % 2
            last_dn = None
            for tb in range(8):
                t_hs = {}
                for ei in range(2):
                    upar = ei
                    if g == 0 and (tb, ei) in pre_tq:
                        t_q = pre_tq[(tb, ei)]
                    else:
                        if g == 0 and (tb, ei) in pre_lastup:
                            last_up = pre_lastup[(tb, ei)]
                        else:
                            last_up = moe_ups(g, tb, ei)
                        t_q = moe_relu_sq(g, tb, ei, last_up)
                    cb_dep = (cbf_toks[g] if g < 2 and tb == 0
                              else cb_toks[g])
                    t_hs[ei] = p.op("vector",
                        lambda e, par=par, ei=ei, tb=tb, upar=upar: e.tensor_tensor(
                            out=hr[upar],
                            in0=hsq[upar],
                            in1=cb[par][:, ei, tb * 256:(tb + 1) * 256]
                                .unsqueeze(1).to_broadcast([128, 4, 256]),
                            op=ALU.mult),
                        [t_q, cb_dep, prev_dn.get(upar)])
                if g == NG - 1 and tb == 7:
                    for q in (0, 1):
                        dep = [tok_shup, tok_xn2T, prev_relu[q]]
                        lu = None
                        for mi in range(4):
                            for hk in range(8):
                                lu = p.op("tensor",
                                    lambda e, q=q, mi=mi, hk=hk: e.matmul(
                                        PS[:, q * 1024:(q + 1) * 1024]
                                        .rearrange("p (m n) -> p m n",
                                                   m=4)[:, mi, :],
                                        shup_sb[:, hk,
                                                q * 512 + mi * 128:
                                                q * 512 + (mi + 1) * 128],
                                        xn2T_sb[:, hk, 0:256],
                                        start=(hk == 0), stop=(hk == 7)), dep)
                        early_sh_up[q] = lu
                for ei in range(2):
                    upar = ei
                    depd = [t_hs[ei], w_toks[g], tok_ao[-1], tok_ctr[-1]]
                    if ei == 0 and acc_evac_last[0] is not None:
                        depd.append(acc_evac_last[0])
                    for ki in range(4):
                        for mt in range(2):
                            for nh in range(2):
                                last_dn = p.op("tensor",
                                    lambda e, par=par, ei=ei, ki=ki, mt=mt,
                                    nh=nh, upar=upar: e.matmul(
                                        ps_acc[:, mt, nh * 512:(nh + 1) * 512],
                                        hr[upar][:, ki, mt * 128:(mt + 1) * 128],
                                        wdn[par][:, ki, ei,
                                                 nh * 512:(nh + 1) * 512],
                                        start=(ei == 0 and ki == 0),
                                        stop=(ei == 1 and ki == 3)), depd)
                    prev_dn[upar] = last_dn
                evs = None
                for mt in range(2):
                    tt = tb * 2 + mt
                    evs = p.op("vector",
                        lambda e, mt=mt, tt=tt: e.scalar_tensor_tensor(
                            out=x_sb[:, tt, :], in0=ps_acc[:, mt, :],
                            scalar=1.0, in1=x_sb[:, tt, :],
                            op0=ALU.mult, op1=ALU.add),
                        [last_dn, ln2_toks[tt]], fence=False)
                acc_evac_last[0] = evs
            w_war[g] = last_dn
            cb_war[g] = last_dn
            if g + 2 < NG:
                load_group_weights(g + 2)
                load_group_cb(g + 2)
            if g == NG - 2:
                al.free("wup0", "wdn0", "cb0")
                shup_sb = al.take("shup", [128, 8, SH_I], BF16)
                tok_shup = p.dma("gpsimd", lambda e: e.dma_start(
                    out=shup_sb,
                    in_=SHUPT.rearrange("(k p) j -> p k j", p=128)),
                    [w_war[g], cb_war[g]], sem="sh")
                shdn_a = al.take("shdna", [128, 4, D], BF16)
                tok_shdn_a = p.dma("gpsimd", lambda e: e.dma_start(
                    out=shdn_a,
                    in_=SHDOWNT[0:512, :].rearrange("(k p) j -> p k j", p=128)),
                    [w_war[g], cb_war[g]], sem="sh2")

        s1 = [al.take("s10", [128, 4, 256], BF16),
              al.take("s11", [128, 4, 256], BF16)]
        al.free("wup1", "wdn1", "cb1", "hr0", "hr1", "hsq0", "hsq1")
        shdn_b = al.take("shdnb", [128, 12, D], BF16)
        tok_shdn_b = p.dma("gpsimd", lambda e: e.dma_start(
            out=shdn_b, in_=SHDOWNT[512:2048, :].rearrange(
                "(k p) j -> p k j", p=128)),
            [w_war[NG - 1], cb_war[NG - 1]], sem="sh3")
        shdn_row = lambda r: shdn_a[:, r, :] if r < 4 else shdn_b[:, r - 4, :]

        ps_s1 = [PS[:, 0:1024].rearrange("p (m n) -> p m n", m=4),
                 PS[:, 1024:2048].rearrange("p (m n) -> p m n", m=4)]
        tok_out = []
        sprev_relu = dict(prev_relu)
        sprev_dn = {}
        last_acc2 = acc_evac_last[0]
        for tb in range(8):
            last_dn = None
            for qp in (0, 2):
                t_qs = {}
                for q in (qp, qp + 1):
                    spar = q % 2
                    if tb == 0 and q in early_sh_up:
                        last_up = early_sh_up[q]
                    else:
                        dep = [tok_shup, tok_xn2T, sprev_relu.get(spar)]
                        last_up = None
                        for mi in range(4):
                            for hk in range(8):
                                last_up = p.op("tensor",
                                    lambda e, q=q, mi=mi, hk=hk, tb=tb,
                                    spar=spar: e.matmul(
                                        ps_s1[spar][:, mi, :],
                                        shup_sb[:, hk,
                                                q * 512 + mi * 128:
                                                q * 512 + (mi + 1) * 128],
                                        xn2T_sb[:, hk,
                                                tb * 256:(tb + 1) * 256],
                                        start=(hk == 0), stop=(hk == 7)), dep)
                    t_r = p.op("scalar", lambda e, spar=spar: e.activation(
                        out=s1[spar].rearrange("p m n -> p (m n)"),
                        in_=ps_s1[spar].rearrange("p m n -> p (m n)"),
                        func=AF.Relu, bias=0.0, scale=1.0),
                        [last_up, sprev_dn.get(spar)], fence=False)
                    sprev_relu[spar] = t_r
                    t_qs[q] = p.op("vector", lambda e, spar=spar: e.tensor_tensor(
                        out=s1[spar].rearrange("p m n -> p (m n)"),
                        in0=s1[spar].rearrange("p m n -> p (m n)"),
                        in1=s1[spar].rearrange("p m n -> p (m n)"),
                        op=ALU.mult), [t_r], fence=False)
                for q in (qp, qp + 1):
                    spar = q % 2
                    depd = [t_qs[q], tok_shdn_a if q == 0 else tok_shdn_b]
                    if q == 0 and last_acc2 is not None:
                        depd.append(last_acc2)
                    for ki in range(4):
                        for mt in range(2):
                            for nh in range(2):
                                last_dn = p.op("tensor",
                                    lambda e, q=q, ki=ki, mt=mt, nh=nh,
                                    spar=spar: e.matmul(
                                        ps_acc[:, mt, nh * 512:(nh + 1) * 512],
                                        s1[spar][:, ki, mt * 128:(mt + 1) * 128],
                                        shdn_row(q * 4 + ki)[:,
                                                nh * 512:(nh + 1) * 512],
                                        start=(q == 0 and ki == 0),
                                        stop=(q == 3 and ki == 3)), depd)
                    sprev_dn[spar] = last_dn
            evs = None
            for mt in range(2):
                tt = tb * 2 + mt
                evs = p.op("vector",
                    lambda e, mt=mt, tt=tt: e.scalar_tensor_tensor(
                        out=x_sb[:, tt, :], in0=ps_acc[:, mt, :], scalar=1.0,
                        in1=x_sb[:, tt, :], op0=ALU.mult, op1=ALU.add),
                    [last_dn], fence=False)
            last_acc2 = evs
            tok_out.append(p.dma("sync", lambda e, tb=tb: e.dma_start(
                out=OUT.rearrange("(a p) h -> p a h", p=128)[:,
                    tb * 2:tb * 2 + 2, :],
                in_=x_sb[:, tb * 2:tb * 2 + 2, :]), [evs], sem="out"))

        p.final_waits = [tok_out[-1]]

        p.emit(block, sems)

    return nc, dbg_taps


def host_inputs(inputs):
    w = {
        "W_IN": np.ascontiguousarray(np.asarray(inputs["L0_attn_W_in"], np.float32)),
        "W_GATE": np.ascontiguousarray(
            np.asarray(inputs["L0_attn_W_gate"], np.float32)),
        "W_OUT": np.ascontiguousarray(
            np.asarray(inputs["L0_attn_W_out"], np.float32)),
        "CT": np.ascontiguousarray(np.asarray(inputs["L0_attn_C"], np.float32).T),
        "DT": np.ascontiguousarray(np.asarray(inputs["L0_attn_D"], np.float32).T),
        "A_N": np.ascontiguousarray(np.asarray(inputs["L0_attn_A"], np.float32)),
        "A_T": np.ascontiguousarray(np.asarray(inputs["L0_attn_A"], np.float32).T),
        "BM_N": np.ascontiguousarray(np.asarray(inputs["L0_attn_Bm"], np.float32)),
        "BM_T": np.ascontiguousarray(
            np.asarray(inputs["L0_attn_Bm"], np.float32).T),
        "B_IN": np.ascontiguousarray(
            np.asarray(inputs["L0_attn_b_in"], np.float32).reshape(2, 128).T),
        "B_GATE": np.ascontiguousarray(
            np.asarray(inputs["L0_attn_b_gate"], np.float32).reshape(8, 128).T),
        "B_OUT": np.ascontiguousarray(
            np.asarray(inputs["L0_attn_b_out"], np.float32).reshape(1, D)),
        "E_BIAS": np.ascontiguousarray(
            np.asarray(inputs["L0_ffn_e_bias"], np.float32).reshape(1, NEXP)),
        "GWT": np.ascontiguousarray(
            np.asarray(inputs["L0_ffn_gate_w"], np.float32).T),
        "UPT": np.ascontiguousarray(
            np.asarray(inputs["L0_ffn_up_w"], np.float32).transpose(0, 2, 1)),
        "DOWNT": np.ascontiguousarray(
            np.asarray(inputs["L0_ffn_down_w"], np.float32).transpose(0, 2, 1)),
        "SHUPT": np.ascontiguousarray(
            np.asarray(inputs["L0_ffn_shared_up_w"], np.float32).T),
        "SHDOWNT": np.ascontiguousarray(
            np.asarray(inputs["L0_ffn_shared_down_w"], np.float32).T),
        "IDENT": np.eye(128, dtype=np.float32),
    }
    x_full = np.ascontiguousarray(np.asarray(inputs["L0_attn_x"], np.float32))
    in_maps = []
    for core in range(8):
        b, half = core // 2, core % 2
        m = dict(w)
        m["X"] = np.ascontiguousarray(x_full[b, half * T_LOC:(half + 1) * T_LOC, :])
        m["XPREV"] = (np.ascontiguousarray(x_full[b, 0:T_LOC, :]) if half == 1
                      else np.zeros((T_LOC, D), np.float32))
        m["FLAG"] = np.full((1, 1), 1.0 if half == 1 else 0.0, np.float32)
        in_maps.append(m)
    return x_full, in_maps


def kernel(**inputs):
    x_full, in_maps = host_inputs(inputs)
    B, T, _ = x_full.shape
    nc, _ = build_program(debug=False)
    res = run_bass_kernel_spmd(nc, in_maps, core_ids=list(range(8)))
    out = np.empty((B, T, D), dtype=np.float32)
    for core in range(8):
        b, half = core // 2, core % 2
        out[b, half * T_LOC:(half + 1) * T_LOC, :] = res.results[core]["OUT"]
    return out


# version 50 (final)
# speedup vs baseline: 1.0237x; 1.0237x over previous
import numpy as np

import concourse.bass as bass
import concourse.mybir as mybir
from concourse.bass_utils import run_bass_kernel_spmd

F32 = mybir.dt.float32
BF16 = mybir.dt.bfloat16
AF = mybir.ActivationFunctionType
ALU = mybir.AluOpType
AX = mybir.AxisListType

NEG_SENT = -1.0e30
KNOCK = -1.0e38

T_LOC = 2048
D = 1024
NST = 256
L1 = 8
NCH = T_LOC // L1
L2 = 16
NB = NCH // L2
NEXP = 32
MOE_I = 512
SH_I = 2048


class Prog:

    ENGS = ("tensor", "vector", "scalar", "sync", "gpsimd")

    def __init__(self):
        self.steps = {e: [] for e in self.ENGS}
        self.cnt = {"t": 0, "v": 0, "s": 0, "p": 0}
        self.pending = {e: None for e in self.ENGS}
        self.final_waits = []

    def _sem_of(self, eng, dma):
        assert not dma
        return {"tensor": "t", "vector": "v", "scalar": "s", "gpsimd": "p"}[eng]

    def _add(self, eng, fn, deps, dma, sem=None, fence=True):
        if not dma:
            sem = self._sem_of(eng, dma)
        self.cnt.setdefault(sem, 0)
        waits = {}
        if self.pending[eng] is not None:
            for s, v in self.pending[eng].items():
                if v > 0:
                    waits[s] = max(waits.get(s, 0), v)
            self.pending[eng] = None
        own = (self._sem_of(eng, False)
               if eng in ("tensor", "vector", "scalar") else None)
        for tok in deps:
            if tok is None:
                continue
            s, v = tok
            if s == own:
                continue
            waits[s] = max(waits.get(s, 0), v)
        if not dma and fence and eng in ("vector", "scalar", "gpsimd"):
            sem_own = self._sem_of(eng, False)
            if self.cnt[sem_own] > 0:
                waits[sem_own] = max(waits.get(sem_own, 0), self.cnt[sem_own])
        inc = 16 if dma else 1
        self.cnt[sem] += inc
        tok = (sem, self.cnt[sem])
        self.steps[eng].append((waits, fn, sem, inc))
        return tok

    def op(self, eng, fn, deps=(), fence=True):
        return self._add(eng, fn, deps, dma=False, fence=fence)

    def dma(self, eng, fn, deps=(), sem=None):
        assert sem is not None
        return self._add(eng, fn, deps, dma=True, sem=sem)

    def barrier(self, engines=None):
        snap = dict(self.cnt)
        for e in (engines if engines is not None else self.ENGS):
            prev = self.pending[e]
            merged = dict(snap)
            if prev:
                for s, v in prev.items():
                    merged[s] = max(merged.get(s, 0), v)
            self.pending[e] = merged

    def emit(self, block, sems):
        def mk(eng_name):
            def body(eng):
                waited = {}
                for waits, fn, sem, inc in self.steps[eng_name]:
                    for s, v in sorted(waits.items()):
                        if waited.get(s, -1) >= v:
                            continue
                        eng.wait_ge(sems[s], v)
                        waited[s] = v
                    fn(eng).then_inc(sems[sem], inc)
                if eng_name == "sync":
                    for s, v in self.final_waits:
                        if waited.get(s, -1) < v:
                            eng.wait_ge(sems[s], v)

            return body

        block.tensor(mk("tensor"))
        block.vector(mk("vector"))
        block.scalar(mk("scalar"))
        block.sync(mk("sync"))
        block.gpsimd(mk("gpsimd"))


class SbufAlloc:

    def __init__(self, nc, base, limit):
        self.nc = nc
        self.free_list = [(base, limit)]
        self.live = {}
        self.n = 0

    def take(self, name, shape, dtype, align=64, side="bottom"):
        esz = {F32: 4, BF16: 2}[dtype]
        per_part = int(np.prod(shape[1:])) * esz
        per_part = (per_part + align - 1) // align * align
        order = (range(len(self.free_list)) if side == "bottom"
                 else range(len(self.free_list) - 1, -1, -1))
        for i in order:
            s, e = self.free_list[i]
            if side == "bottom":
                s_al = (s + align - 1) // align * align
            else:
                s_al = (e - per_part) // align * align
                if s_al < s:
                    continue
            if s_al + per_part <= e and s_al >= s:
                self.free_list.pop(i)
                if s_al + per_part < e:
                    self.free_list.insert(i, (s_al + per_part, e))
                if s_al > s:
                    self.free_list.insert(i, (s, s_al))
                self.free_list.sort()
                self.n += 1
                h = self.nc.alloc_sbuf_tensor_at(
                    f"{name}_{self.n}", list(shape), dtype, offset=s_al)
                self.live[name] = (s_al, s_al + per_part)
                return h.ap()
        raise MemoryError(
            f"SBUF alloc failed for {name}: {per_part}B, free={self.free_list}")

    def free(self, *names):
        for name in names:
            self.free_list.append(self.live.pop(name))
        self.free_list.sort()
        out = []
        for s, e in self.free_list:
            if out and out[-1][1] >= s:
                out[-1] = (out[-1][0], max(out[-1][1], e))
            else:
                out.append([s, e])
        self.free_list = [tuple(x) for x in out]


def build_program(debug=False):
    nc = bass.Bass("TRN2", target_bir_lowering=False)

    def din(name, shape):
        return nc.dram_tensor(name, list(shape), F32, kind="ExternalInput").ap()

    X = din("X", [T_LOC, D])
    XPREV = din("XPREV", [T_LOC, D])
    FLAG = din("FLAG", [1, 1])
    W_IN = din("W_IN", [D, NST])
    W_GATE = din("W_GATE", [D, D])
    W_OUT = din("W_OUT", [D, D])
    CT = din("CT", [NST, D])
    DT = din("DT", [NST, D])
    A_N = din("A_N", [NST, NST])
    A_T = din("A_T", [NST, NST])
    BM_N = din("BM_N", [NST, NST])
    BM_T = din("BM_T", [NST, NST])
    B_IN = din("B_IN", [128, 2])
    B_GATE = din("B_GATE", [128, 8])
    B_OUT = din("B_OUT", [1, D])
    E_BIAS = din("E_BIAS", [1, NEXP])
    GWT = din("GWT", [D, NEXP])
    UPT = din("UPT", [NEXP, D, MOE_I])
    DOWNT = din("DOWNT", [NEXP, MOE_I, D])
    SHUPT = din("SHUPT", [D, SH_I])
    SHDOWNT = din("SHDOWNT", [SH_I, D])
    IDENT = din("IDENT", [128, 128])

    OUT = nc.dram_tensor("OUT", [T_LOC, D], F32, kind="ExternalOutput").ap()
    CSCR = nc.dram_tensor("CSCR", [NEXP, T_LOC], BF16, kind="Internal").ap()

    dbg_taps = {}

    def dbg_tap(name, shape, dtype=F32):
        ap = nc.dram_tensor("DBG_" + name, list(shape), dtype,
                            kind="ExternalOutput").ap()
        dbg_taps[name] = ap
        return ap

    p = Prog()
    r256 = lambda t: t.rearrange("(k p) n -> p k n", p=128)

    al = SbufAlloc(nc, base=16640, limit=229248)
    x_sb = al.take("x", [128, 16, D], F32)
    ident_sb = al.take("ident", [128, 128], BF16)
    bout_sb = al.take("boutb", [128, D], F32)
    ebias_sb = al.take("ebias", [128, NEXP], F32)
    flag_sb = al.take("flag", [128, 1], F32)
    gwt_sb = al.take("gwt", [128, 8, NEXP], BF16)
    bin_sb = al.take("binpc", [128, 2], F32)
    bgate_sb = al.take("bgatepc", [128, 8], F32)
    ct_sb = al.take("cT", [32, 16, 128], BF16)
    stats_sb = al.take("stats", [128, 2, 6], F32)
    mv_sb = al.take("mv", [128, 16, 2], F32)
    rstd_sb = al.take("rstd", [128, 16], F32)
    nmr_sb = al.take("nmr", [128, 16], F32)
    r1_sb = al.take("r1", [128, 1], F32)
    eps_sb = al.take("eps", [128, 1], F32)
    mv2_sb = al.take("mv2", [128, 16, 2, 2], F32)
    tq0_sb = al.take("tq0", [128, 16], F32)
    tq1_sb = al.take("tq1", [128, 16], F32)
    r2_sb = al.take("r2", [128, 1], F32)
    sel_all_sb = al.take("sel_all", [128, 16, NEXP], F32)
    rden_sb = al.take("rden", [128, 16], F32)
    tmp8_sb = al.take("tmp8", [128, 16, 8], F32)
    gs_sb = al.take("gs", [128, 16, 8], F32)
    st_sb = al.take("st", [128, 2, 1], BF16)
    pprev_sb = al.take("pprev", [128, 2, 1], F32)
    k3_sb = al.take("k3", [128, 2, L2], BF16)
    h2_sb = al.take("h2", [128, 2, L2, NB], F32)
    carry_sb = al.take("carry", [128, 2, L2, NB], BF16)

    ps_h = nc.alloc_psum_tensor("PS", [128, 4096], F32)
    PS = ps_h.ap() if hasattr(ps_h, "ap") else ps_h

    from contextlib import ExitStack
    _sem_names = ["t", "v", "s", "p", "c0", "small", "x", "xp", "wattn",
                  "w0", "w1", "cb0", "cb1", "cf0", "cf1", "cscr", "c2",
                  "sh", "sh2", "sh3", "out"]
    with ExitStack() as _sctx:
        sems = {n: _sctx.enter_context(nc.semaphore("sem_" + n))
                for n in _sem_names}
        block = _sctx.enter_context(nc.Block())

        x_view = X.rearrange("(a p) h -> p a h", p=128)
        tok_x_q = []
        for q in range(4):
            tok_x_q.append(p.dma("sync", lambda e, q=q: e.dma_start(
                out=x_sb[:, 4 * q:4 * (q + 1), :],
                in_=x_view[:, 4 * q:4 * (q + 1), :]), sem="x"))
        tok_x = tok_x_q[-1]
        p.dma("gpsimd", lambda e: e.dma_start(out=ident_sb, in_=IDENT),
              sem="c0")
        p.dma("sync", lambda e: e.dma_start(
            out=bout_sb, in_=B_OUT.to_broadcast([128, D])), sem="small")
        p.dma("sync", lambda e: e.dma_start(
            out=ebias_sb, in_=E_BIAS.to_broadcast([128, NEXP])), sem="small")
        p.dma("sync", lambda e: e.dma_start(
            out=flag_sb, in_=FLAG.to_broadcast([128, 1])), sem="small")
        tok_gwt = p.dma("gpsimd", lambda e: e.dma_start(
            out=gwt_sb, in_=GWT.rearrange("(k p) n -> p k n", p=128)), sem="c0")
        p.dma("sync", lambda e: e.dma_start(out=bin_sb, in_=B_IN), sem="small")
        tok_small = p.dma("sync", lambda e: e.dma_start(
            out=bgate_sb, in_=B_GATE), sem="small")
        tok_bout = tok_ebias = tok_flag = tok_bin = tok_bgate = tok_small

        a_nat = al.take("a_nat", [128, 2, NST], BF16, side="top")
        a_t = al.take("a_t", [128, 2, NST], BF16, side="top")
        bm_nat = al.take("bm_nat", [128, 2, NST], BF16, side="top")
        bm_t = al.take("bm_t", [128, 2, NST], BF16, side="top")
        alt_sb = al.take("alt", [128, 2, NST], BF16, side="top")
        apow = [None, a_nat] + [
            al.take(f"apow{k}", [128, 2, NST], BF16, side="top") for k in range(2, L1 + 1)]
        gmat = [bm_nat] + [
            al.take(f"g{k}", [128, 2, NST], BF16, side="top") for k in range(1, L1)]
        alpow = [None, None] + [
            al.take(f"alpow{k}", [128, 2, NST], BF16, side="top") for k in range(2, L2 + 1)]
        win_sb = al.take("win", [128, 8, NST], BF16)
        wg_sb = al.take("wg", [128, 8, D], BF16)

        p.dma("gpsimd", lambda e: e.dma_start(out=a_nat, in_=r256(A_N)),
              sem="c0")
        p.dma("gpsimd", lambda e: e.dma_start(out=a_t, in_=r256(A_T)), sem="c0")
        p.dma("gpsimd", lambda e: e.dma_start(out=bm_nat, in_=r256(BM_N)),
              sem="c0")
        tok_c0 = p.dma("gpsimd", lambda e: e.dma_start(
            out=bm_t, in_=r256(BM_T)), sem="c0")
        tok_a = tok_at = tok_bm = tok_bmt = tok_ident = tok_gwt = tok_c0
        tok_win = p.dma("gpsimd", lambda e: e.dma_start(
            out=win_sb, in_=r256(W_IN)), sem="wattn")
        tok_wg0 = p.dma("gpsimd", lambda e: e.dma_start(
            out=wg_sb, in_=r256(W_GATE)), sem="wattn")
        tok_win = tok_wg0

        tok_eps = p.op("vector", lambda e: e.memset(eps_sb, 1e-5), [])

        PS_mat = PS[:, 3072:3584].rearrange("p (m n) -> p m n", m=2)

        def mat_mm(dst_sb, lhsT_sb, rhs_sb, deps):
            last = None
            for m in range(2):
                for nt in range(2):
                    last = p.op("tensor", lambda e, m=m, nt=nt: e.matmul(
                        PS_mat[:, m, :],
                        lhsT_sb[:, nt, m * 128:(m + 1) * 128],
                        rhs_sb[:, nt, :],
                        start=(nt == 0), stop=(nt == 1)), deps)
            return p.op("scalar", lambda e: e.activation(
                out=dst_sb.rearrange("p m n -> p (m n)"),
                in_=PS_mat.rearrange("p m n -> p (m n)"),
                func=AF.Copy, bias=0.0, scale=1.0), [last], fence=False)

        dep0 = [tok_a, tok_at, tok_bm, tok_bmt]
        tok_pow = [None, tok_a]
        prev = tok_a
        for k in range(2, L1 + 1):
            prev = mat_mm(apow[k], a_t, apow[k - 1], dep0 + [prev])
            tok_pow.append(prev)
        tok_g = [tok_bm]
        tg = prev
        for k in range(1, L1):
            tg = mat_mm(gmat[k], bm_t, apow[k], dep0 + [tok_pow[k], tg])
            tok_g.append(tg)
        PS_tr = PS[:, 3584:4096].rearrange("p (m n) -> p m n", m=2)
        last = None
        for nt in range(2):
            for m in range(2):
                last = p.op("tensor", lambda e, nt=nt, m=m: e.matmul(
                    PS_tr[:, m, nt * 128:(nt + 1) * 128],
                    apow[L1][:, nt, m * 128:(m + 1) * 128],
                    ident_sb, start=True, stop=True),
                    [tok_pow[L1], tok_ident])
        tok_alt = p.op("scalar", lambda e: e.activation(
            out=alt_sb.rearrange("p m n -> p (m n)"),
            in_=PS_tr.rearrange("p m n -> p (m n)"),
            func=AF.Copy, bias=0.0, scale=1.0), [last], fence=False)
        alpow[1] = apow[L1]
        tok_alpow = [None, tok_pow[L1]]
        prev = tok_alt
        for k in range(2, L2 + 1):
            prev = mat_mm(alpow[k], alt_sb, alpow[k - 1],
                          [tok_alt, prev, tok_alpow[k - 1]])
            tok_alpow.append(prev)
        xpq = [al.take(f"xpq{q}", [128, 8, D], BF16, side="top")
               for q in range(2)]
        al.free("a_t", "bm_t", "alt")

        ln_chain = [None]

        def emit_ln_stats(src_sb, tt, dep, prev_ln=None, fence=True):
            p.op("vector", lambda e, tt=tt: e.bn_stats(
                out=stats_sb[:, 0, :], in_=src_sb[:, tt, 0:512]),
                [dep, prev_ln], fence=fence)
            p.op("vector", lambda e, tt=tt: e.bn_stats(
                out=stats_sb[:, 1, :], in_=src_sb[:, tt, 512:1024]),
                [dep, prev_ln], fence=False)
            p.op("vector", lambda e, tt=tt: e.bn_aggr(
                out=mv2_sb[:, tt, 0, :], in_=stats_sb[:, 0, :]), [])
            p.op("vector", lambda e, tt=tt: e.bn_aggr(
                out=mv2_sb[:, tt, 1, :], in_=stats_sb[:, 1, :]), [])

        def emit_ln(src_sb, dst_sb, ntt, dep_in, extra_war=(), per_tile_dep=None,
                    norm_eng="scalar", stats_done=False):
            prev_ln = ln_chain[0]
            if not stats_done:
                for tt in range(ntt):
                    dtt = per_tile_dep[tt] if per_tile_dep else dep_in
                    emit_ln_stats(src_sb, tt, dtt, prev_ln)
            m0 = mv2_sb[:, 0:ntt, 0, 0]
            v0 = mv2_sb[:, 0:ntt, 0, 1]
            m1 = mv2_sb[:, 0:ntt, 1, 0]
            v1 = mv2_sb[:, 0:ntt, 1, 1]
            p.op("vector", lambda e: e.tensor_tensor(
                out=tq0_sb[:, 0:ntt], in0=m0, in1=m1, op=ALU.add), [])
            p.op("vector", lambda e: e.tensor_single_scalar(
                out=mv_sb[:, 0:ntt, 0], in_=tq0_sb[:, 0:ntt], scalar=0.5,
                op=ALU.mult), [])
            p.op("vector", lambda e: e.tensor_tensor(
                out=tq0_sb[:, 0:ntt], in0=m0, in1=m1, op=ALU.subtract), [])
            p.op("vector", lambda e: e.tensor_single_scalar(
                out=tq0_sb[:, 0:ntt], in_=tq0_sb[:, 0:ntt], scalar=0.5,
                op=ALU.mult), [])
            p.op("vector", lambda e: e.tensor_tensor(
                out=tq0_sb[:, 0:ntt], in0=tq0_sb[:, 0:ntt],
                in1=tq0_sb[:, 0:ntt], op=ALU.mult), [])
            p.op("vector", lambda e: e.tensor_tensor(
                out=tq1_sb[:, 0:ntt], in0=v0, in1=v1, op=ALU.add), [])
            p.op("vector", lambda e: e.tensor_single_scalar(
                out=tq1_sb[:, 0:ntt], in_=tq1_sb[:, 0:ntt], scalar=0.5,
                op=ALU.mult), [])
            t2 = p.op("vector", lambda e: e.tensor_tensor(
                out=mv_sb[:, 0:ntt, 1], in0=tq1_sb[:, 0:ntt],
                in1=tq0_sb[:, 0:ntt], op=ALU.add), [])
            t3 = p.op("scalar", lambda e: e.activation(
                out=rstd_sb[:, :ntt], in_=mv_sb[:, :ntt, 1],
                func=AF.Ln, bias=eps_sb, scale=1.0), [t2, tok_eps])
            t4 = p.op("scalar", lambda e: e.activation(
                out=rstd_sb[:, :ntt], in_=rstd_sb[:, :ntt],
                func=AF.Exp, bias=0.0, scale=-0.5), [t3])
            t5 = p.op("vector", lambda e: e.scalar_tensor_tensor(
                out=nmr_sb[:, :ntt], in0=mv_sb[:, :ntt, 0], scalar=-1.0,
                in1=rstd_sb[:, :ntt], op0=ALU.mult, op1=ALU.mult), [t4, t2])
            toks = []
            first_s = True
            for tt in range(ntt):
                dtt = per_tile_dep[tt] if per_tile_dep else dep_in
                eng_v = (norm_eng == "vector" or
                         (norm_eng == "mixed" and tt < ntt - 4))
                if eng_v:
                    toks.append(p.op("vector", lambda e, tt=tt: e.tensor_scalar(
                        out=dst_sb[:, tt, :], in0=src_sb[:, tt, :],
                        scalar1=rstd_sb[:, tt:tt + 1],
                        scalar2=nmr_sb[:, tt:tt + 1],
                        op0=ALU.mult, op1=ALU.add),
                        [t5, dtt] + list(extra_war), fence=(tt == 0)))
                else:
                    toks.append(p.op("scalar", lambda e, tt=tt: e.activation(
                        out=dst_sb[:, tt, :], in_=src_sb[:, tt, :],
                        func=AF.Identity, bias=nmr_sb[:, tt:tt + 1],
                        scale=rstd_sb[:, tt:tt + 1]),
                        [t5, dtt] + list(extra_war),
                        fence=(tt == 0 or (norm_eng == "mixed" and first_s))))
                    if norm_eng == "mixed":
                        first_s = False
            ln_chain[0] = toks[-1]
            return toks

        tr_evacs = []

        def emit_transpose(src_sb, dst_sb, ntt, dep_toks, extra_war=()):
            evacs = tr_evacs
            step = len(tr_evacs)
            for grp in range(0, ntt, 4):
                g = min(4, ntt - grp)
                for hk in range(8):
                    par = step % 2
                    ps = PS[:, par * 512: par * 512 + 512]
                    dep = [tok_ident] + list(extra_war)
                    dep.append(dep_toks[min(grp + g - 1, len(dep_toks) - 1)])
                    if len(evacs) >= 2:
                        dep.append(evacs[-2])
                    last = None
                    for j in range(g):
                        tt = grp + j
                        last = p.op("tensor",
                            lambda e, j=j, tt=tt, hk=hk, ps=ps: e.matmul(
                                ps[:, j * 128:(j + 1) * 128],
                                src_sb[:, tt, hk * 128:(hk + 1) * 128],
                                ident_sb, start=True, stop=True), dep)
                    evacs.append(p.op("scalar",
                        lambda e, hk=hk, grp=grp, g=g, ps=ps: e.activation(
                            out=dst_sb[:, hk, grp * 128:(grp + g) * 128],
                            in_=ps[:, : g * 128],
                            func=AF.Copy, bias=0.0, scale=1.0), [last],
                        fence=False))
                    step += 1
            return evacs

        def emit_lvl23(p_src, dep_p, init_tok, want_carry):
            PS_h2 = PS[:, 2048:2560].rearrange("p (m n) -> p m n", m=2)
            last = None
            for m in range(2):
                first = True
                for k in range(L2):
                    for nt in range(2):
                        rhs = p_src[:, nt, 0:L2 - k, :].rearrange(
                            "p a b -> p (a b)")
                        out = PS_h2[:, m, k * NB:]
                        if k == 0:
                            if nt != m:
                                continue
                            last = p.op("tensor",
                                lambda e, rhs=rhs, out=out, first=first: e.matmul(
                                    out, ident_sb, rhs,
                                    start=first, stop=False),
                                [dep_p, tok_ident])
                        else:
                            last = p.op("tensor",
                                lambda e, k=k, nt=nt, m=m, rhs=rhs, out=out,
                                first=first: e.matmul(
                                    out, alpow[k][:, nt, m * 128:(m + 1) * 128],
                                    rhs, start=first,
                                    stop=(k == L2 - 1 and nt == 1)),
                                [dep_p, tok_alpow[k]])
                        first = False
            tok_h2 = p.op("scalar", lambda e: e.activation(
                out=h2_sb.rearrange("p m a b -> p (m a b)"),
                in_=PS_h2.rearrange("p m n -> p (m n)"),
                func=AF.Copy, bias=0.0, scale=1.0), [last], fence=False)
            PS_st = PS[:, 3072:3074].rearrange("p (m n) -> p m n", m=2)
            tok_prev = [tok_h2, init_tok]
            k3_last = None
            for b in range(NB):
                if want_carry:
                    k3_last = p.op("vector", lambda e, b=b: e.tensor_copy(
                        out=k3_sb[:, :, b:b + 1], in_=st_sb), tok_prev)
                last = None
                for m in range(2):
                    for nt in range(2):
                        last = p.op("tensor", lambda e, m=m, nt=nt: e.matmul(
                            PS_st[:, m, :],
                            alpow[L2][:, nt, m * 128:(m + 1) * 128],
                            st_sb[:, nt, :],
                            start=(nt == 0), stop=(nt == 1)),
                            tok_prev + [tok_alpow[L2]])
                tok_st = p.op("vector", lambda e, b=b: e.tensor_tensor(
                    out=st_sb.rearrange("p m n -> p (m n)"),
                    in0=PS_st.rearrange("p m n -> p (m n)"),
                    in1=h2_sb[:, :, L2 - 1, b],
                    op=ALU.add), [last, tok_h2])
                tok_prev = [tok_st]
            if not want_carry:
                return tok_prev[0]
            PS_c2 = PS[:, 2560:3072].rearrange("p (m o b) -> p m o b",
                                               m=2, o=L2)
            last = None
            for o in range(L2):
                for m in range(2):
                    for nt in range(2):
                        last = p.op("tensor",
                            lambda e, o=o, m=m, nt=nt: e.matmul(
                                PS_c2[:, m, o, :],
                                alpow[o + 1][:, nt, m * 128:(m + 1) * 128],
                                k3_sb[:, nt, :],
                                start=(nt == 0), stop=(nt == 1)),
                            [k3_last, tok_alpow[o + 1], tok_prev[0]])
            tok_c1 = p.op("vector", lambda e: e.tensor_tensor(
                out=carry_sb[:, :, 1:L2, :], in0=PS_c2[:, :, 0:L2 - 1, :],
                in1=h2_sb[:, :, 0:L2 - 1, :], op=ALU.add),
                [last, tok_h2], fence=False)
            tok_c2 = p.op("vector", lambda e: e.tensor_tensor(
                out=carry_sb[:, :, 0:1, 1:NB],
                in0=PS_c2[:, :, L2 - 1:L2, 0:NB - 1],
                in1=h2_sb[:, :, L2 - 1:L2, 0:NB - 1], op=ALU.add),
                [last], fence=False)
            tok_c0 = p.op("vector", lambda e: e.tensor_copy(
                out=carry_sb[:, :, 0, 0:1], in_=pprev_sb.rearrange(
                    "p m n -> p m n")), [tok_c2, init_tok])
            return tok_c0

        u_prev = al.take("u_prev", [128, 2, T_LOC], BF16, side="top")
        xnpT_sb = al.take("xnpT_h", [128, 8, T_LOC], BF16, side="top")

        tok_u_prev = []
        xp_view = XPREV.rearrange("(a p) h -> p a h", p=128)
        for q in range(2):
            tok_xp = p.dma("gpsimd", lambda e, q=q: e.dma_start(
                out=xpq[q], in_=xp_view[:, 8 * q:8 * (q + 1), :]), sem="xp")
        for q in range(2):
            ln_toks = emit_ln(xpq[q], xpq[q], 8, tok_xp, norm_eng="vector")
            n0 = len(tr_evacs)
            tr_toks = emit_transpose(
                xpq[q], xnpT_sb[:, :, q * 1024:(q + 1) * 1024], 8, ln_toks,
                extra_war=([tok_u_prev[-1]] if tok_u_prev else []))
            trl = tr_toks[n0:]
            for tcl in range(2):
                tc = q * 2 + tcl
                for nt in range(2):
                    par = (tc * 2 + nt) % 2
                    ps = PS[:, 1024 + par * 512: 1536 + par * 512]
                    last = None
                    dep = [trl[tcl * 8 + 7], tok_win]
                    if len(tok_u_prev) >= 2:
                        dep.append(tok_u_prev[-2])
                    for hk in range(8):
                        last = p.op("tensor",
                            lambda e, nt=nt, tc=tc, hk=hk, ps=ps: e.matmul(
                                ps, win_sb[:, hk, nt * 128:(nt + 1) * 128],
                                xnpT_sb[:, hk, tc * 512:(tc + 1) * 512],
                                start=(hk == 0), stop=(hk == 7)), dep)
                    tok_u_prev.append(p.op("scalar",
                        lambda e, nt=nt, tc=tc, ps=ps: e.activation(
                            out=u_prev[:, nt, tc * 512:(tc + 1) * 512],
                            in_=ps, func=AF.Identity,
                            bias=bin_sb[:, nt:nt + 1], scale=1.0),
                        [last, tok_bin], fence=False))
        PS_pp = PS[:, 2048:2560].rearrange("p (m n) -> p m n", m=2)
        last = None
        for m in range(2):
            first = True
            for l in range(L1):
                for nt in range(2):
                    last = p.op("tensor",
                        lambda e, m=m, l=l, nt=nt, first=first: e.matmul(
                            PS_pp[:, m, :],
                            gmat[L1 - 1 - l][:, nt, m * 128:(m + 1) * 128],
                            u_prev[:, nt, :].rearrange(
                                "p (c l) -> p c l", l=L1)[:, :, l],
                            start=first, stop=(l == L1 - 1 and nt == 1)),
                        [tok_u_prev[-1], tok_g[L1 - 1 - l]])
                    first = False
        pp_sb = al.take("pp", [128, 2, L2, NB], BF16, side="top")
        tok_pp = p.op("scalar", lambda e: e.activation(
            out=pp_sb.rearrange("p m c b -> p m b c"),
            in_=PS_pp,
            func=AF.Copy, bias=0.0, scale=1.0), [last], fence=False)

        al.free("xpq0", "xpq1", "xnpT_h")

        xn_sb = al.take("xn", [128, 16, D], BF16, side="top")
        xnT_sb = al.take("xnT", [128, 8, T_LOC], BF16)
        ln1_toks = emit_ln(x_sb, xn_sb, 16, tok_x, norm_eng="mixed",
                           extra_war=[tok_u_prev[-1]])
        tr1_toks = emit_transpose(xn_sb, xnT_sb, 16, ln1_toks)
        tok_xnT = tr1_toks[-1]
        if debug:
            d = dbg_tap("xnT", [128, 8 * T_LOC], BF16)
            p.dma("sync", lambda e, d=d: e.dma_start(
                out=d, in_=xnT_sb.rearrange("p a b -> p (a b)")), [tok_xnT], sem="out")

        al.free("xn")

        u_nat = al.take("u_nat", [128, 2, T_LOC], BF16)
        u_int = al.take("u_int", [128, 2, L1, NCH], BF16, side="top")
        tok_unat = []
        for nt in range(2):
            for tc in range(4):
                par = (nt * 4 + tc) % 2
                ps = PS[:, par * 512:(par + 1) * 512]
                dep = [tok_xnT, tok_win]
                if len(tok_unat) >= 2:
                    dep.append(tok_unat[-2])
                last = None
                for hk in range(8):
                    last = p.op("tensor",
                        lambda e, nt=nt, tc=tc, hk=hk, ps=ps: e.matmul(
                            ps, win_sb[:, hk, nt * 128:(nt + 1) * 128],
                            xnT_sb[:, hk, tc * 512:(tc + 1) * 512],
                            start=(hk == 0), stop=(hk == 7)), dep)
                tok_unat.append(p.op("scalar",
                    lambda e, nt=nt, tc=tc, ps=ps: e.activation(
                        out=u_nat[:, nt, tc * 512:(tc + 1) * 512], in_=ps,
                        func=AF.Identity, bias=bin_sb[:, nt:nt + 1], scale=1.0),
                    [last, tok_bin], fence=False))
        tok_uint = []
        for nt in range(2):
            tok_uint.append(p.op("vector", lambda e, nt=nt: e.tensor_copy(
                out=u_int[:, nt, :, :].rearrange("p l c -> p c l"),
                in_=u_nat[:, nt, :]), [tok_unat[-1]], fence=False))
        if debug:
            d = dbg_tap("u_nat", [128, 2 * T_LOC], BF16)
            p.dma("sync", lambda e, d=d: e.dma_start(
                out=d, in_=u_nat.rearrange("p a b -> p (a b)")), [tok_unat[-1]], sem="out")

        tok_z = p.op("vector", lambda e: e.memset(
            st_sb.rearrange("p m n -> p (m n)"), 0.0), [])
        tok_prev_final = emit_lvl23(pp_sb, tok_pp, tok_z, want_carry=False)
        tok_pprev = p.op("vector", lambda e: e.scalar_tensor_tensor(
            out=pprev_sb.rearrange("p m n -> p (m n)"),
            in0=st_sb.rearrange("p m n -> p (m n)"),
            scalar=1.0, in1=flag_sb.to_broadcast([128, 2]),
            op0=ALU.mult, op1=ALU.mult), [tok_prev_final, tok_flag])
        if debug:
            d = dbg_tap("pprev", [128, 2])
            p.dma("sync", lambda e, d=d: e.dma_start(
                out=d, in_=pprev_sb.rearrange("p m n -> p (m n)")),
                [tok_pprev], sem="out")
        al.free("pp")

        S_int = al.take("S_int", [128, 2, L1, NCH], BF16, side="top")
        hs_sb = al.take("hs", [128, 2, T_LOC], BF16)
        tok_Sevac = []
        for tb in range(4):
            par = tb % 2
            ps = PS[:, par * 1024:(par + 1) * 1024].rearrange(
                "p (m l c) -> p m l c", m=2, l=L1)
            last = None
            for m in range(2):
                first = True
                for k in range(L1):
                    for nt in range(2):
                        dep = [tok_uint[nt], tok_g[k]]
                        if len(tok_Sevac) >= 2:
                            dep.append(tok_Sevac[-2])
                        last = p.op("tensor",
                            lambda e, m=m, k=k, nt=nt, tb=tb, ps=ps,
                            first=first: e.matmul(
                                ps[:, m, k:L1, :].rearrange("p a b -> p (a b)"),
                                gmat[k][:, nt, m * 128:(m + 1) * 128],
                                u_int[:, nt, 0:L1 - k, tb * 64:(tb + 1) * 64],
                                start=first,
                                stop=(k == L1 - 1 and nt == 1)), dep)
                        first = False
            tok_Sevac.append(p.op("scalar", lambda e, tb=tb, ps=ps: e.activation(
                out=S_int[:, :, :, tb * 64:(tb + 1) * 64],
                in_=ps, func=AF.Copy, bias=0.0, scale=1.0), [last],
                fence=False))
        tok_S = tok_Sevac[-1]
        tok_init = p.op("vector", lambda e: e.tensor_copy(
            out=st_sb, in_=pprev_sb), [tok_pprev])
        pf_sb = al.take("pf", [128, 2, L2, NB], BF16, side="top")
        tok_pf = p.op("vector", lambda e: e.tensor_copy(
            out=pf_sb,
            in_=S_int[:, :, L1 - 1, :].rearrange(
                "p m (b c) -> p m c b", c=L2)), [tok_S], fence=False)
        tok_carry = emit_lvl23(pf_sb, tok_pf, tok_init, want_carry=True)
        tok_hs = []
        corr_mm_last = []
        for o in range(L1):
            base = o * 512
            PS_c = PS[:, base:base + 512].rearrange("p (m n) -> p m n", m=2)
            last = None
            for m in range(2):
                for nt in range(2):
                    dep = [tok_carry, tok_pow[o + 1]]
                    last = p.op("tensor",
                        lambda e, o=o, m=m, nt=nt, PS_c=PS_c: e.matmul(
                            PS_c[:, m, :],
                            apow[o + 1][:, nt, m * 128:(m + 1) * 128],
                            carry_sb[:, nt, :, :].rearrange("p a b -> p (a b)"),
                            start=(nt == 0), stop=False), dep)
                last = p.op("tensor",
                    lambda e, o=o, m=m, PS_c=PS_c: e.matmul(
                        PS_c[:, m, :], ident_sb,
                        S_int[:, m, o, :].rearrange("p (b c) -> p c b", c=L2),
                        start=False, stop=True), [tok_S, tok_ident])
            corr_mm_last.append(last)
        for o in range(L1):
            base = o * 512
            PS_c = PS[:, base:base + 512].rearrange("p (m n) -> p m n", m=2)
            tok_hs.append(p.op("vector", lambda e, o=o, PS_c=PS_c: e.tensor_copy(
                out=hs_sb.rearrange(
                    "p m (b c l) -> p m c b l", c=L2, l=L1)[:, :, :, :, o],
                in_=PS_c.rearrange("p m (c b) -> p m c b", b=NB)),
                [corr_mm_last[o]], fence=False))
        tok_hs_done = tok_hs[-1]
        if debug:
            d = dbg_tap("hs", [128, 2 * T_LOC], BF16)
            p.dma("sync", lambda e, d=d: e.dma_start(
                out=d, in_=hs_sb.rearrange("p a b -> p (a b)")), [tok_hs_done], sem="out")
        tok_xb = []
        for tt in range(16):
            tok_xb.append(p.op("vector", lambda e, tt=tt: e.tensor_tensor(
                out=x_sb[:, tt, :], in0=x_sb[:, tt, :], in1=bout_sb,
                op=ALU.add), [ln1_toks[tt], tok_bout], fence=False))

        for k in range(2, L1 + 1):
            al.free(f"apow{k}")
        for k in range(1, L1):
            al.free(f"g{k}")
        for k in range(2, L2 + 1):
            al.free(f"alpow{k}")
        al.free("a_nat", "bm_nat", "pf")
        gate_sb = al.take("gate", [128, 8, T_LOC], BF16, side="top")
        al.free("S_int", "u_int", "u_prev", "win")
        al.free("h2", "carry", "k3", "st", "pprev")

        wo_sb = al.take("wo", [128, 8, D], BF16)
        ctw_sb = al.take("ctw", [128, 2, D], BF16)
        dtw_sb = al.take("dtw", [128, 2, D], BF16)
        wdep = [tok_hs_done, corr_mm_last[-1]]
        p.dma("gpsimd", lambda e: e.dma_start(out=wo_sb, in_=r256(W_OUT)),
              wdep, sem="wattn")
        p.dma("gpsimd", lambda e: e.dma_start(out=ctw_sb, in_=r256(CT)),
              wdep, sem="wattn")
        tok_wattn2 = p.dma("gpsimd", lambda e: e.dma_start(
            out=dtw_sb, in_=r256(DT)), wdep, sem="wattn")
        tok_wo = tok_ct = tok_dt = tok_wattn2
        tok_wg = tok_win
        tok_gate = []
        tok_gate_dmtc = {}
        for dm in range(8):
            for tc in range(4):
                par = (dm * 4 + tc) % 2
                ps = PS[:, par * 512:(par + 1) * 512]
                dep = [tok_xnT, tok_wg, tok_S]
                if len(tok_gate) >= 2:
                    dep.append(tok_gate[-2])
                else:
                    dep.append(tok_hs[par])
                last = None
                for hk in range(8):
                    last = p.op("tensor",
                        lambda e, dm=dm, tc=tc, hk=hk, ps=ps: e.matmul(
                            ps, wg_sb[:, hk, dm * 128:(dm + 1) * 128],
                            xnT_sb[:, hk, tc * 512:(tc + 1) * 512],
                            start=(hk == 0), stop=(hk == 7)), dep)
                sdep = [last, tok_bgate]
                if len(tok_gate) < 2:
                    sdep.append(corr_mm_last[-1])
                tok_gate.append(p.op("scalar",
                    lambda e, dm=dm, tc=tc, ps=ps: e.activation(
                        out=gate_sb[:, dm, tc * 512:(tc + 1) * 512], in_=ps,
                        func=AF.Sigmoid, bias=bgate_sb[:, dm:dm + 1], scale=1.0),
                    sdep, fence=False))
                tok_gate_dmtc[(dm, tc)] = tok_gate[-1]
        if debug:
            d = dbg_tap("gate", [128, 8 * T_LOC], BF16)
            p.dma("sync", lambda e, d=d: e.dma_start(
                out=d, in_=gate_sb.rearrange("p a b -> p (a b)")), [tok_gate[-1]], sem="out")

        p.barrier()
        al.free("xnT", "wg")

        yT_sb = al.take("yT", [128, 8, T_LOC], BF16, side="top")
        tok_y = []
        tok_y_tc = {}
        for tc in range(4):
            for dm in range(8):
                par = (tc * 8 + dm) % 2
                ps = PS[:, 1024 + par * 512:1536 + par * 512]
                dep = [tok_hs_done, tok_ct, tok_unat[-1], tok_dt]
                if len(tok_y) >= 2:
                    dep.append(tok_y[-2])
                last = None
                for nt in range(2):
                    last = p.op("tensor",
                        lambda e, dm=dm, tc=tc, nt=nt, ps=ps: e.matmul(
                            ps, ctw_sb[:, nt, dm * 128:(dm + 1) * 128],
                            hs_sb[:, nt, tc * 512:(tc + 1) * 512],
                            start=(nt == 0), stop=False), dep)
                for nt in range(2):
                    last = p.op("tensor",
                        lambda e, dm=dm, tc=tc, nt=nt, ps=ps: e.matmul(
                            ps, dtw_sb[:, nt, dm * 128:(dm + 1) * 128],
                            u_nat[:, nt, tc * 512:(tc + 1) * 512],
                            start=False, stop=(nt == 1)), [tok_dt])
                tok_y.append(p.op("vector",
                    lambda e, dm=dm, tc=tc, ps=ps: e.tensor_tensor(
                        out=yT_sb[:, dm, tc * 512:(tc + 1) * 512],
                        in0=ps, in1=gate_sb[:, dm, tc * 512:(tc + 1) * 512],
                        op=ALU.mult), [last, tok_gate_dmtc[(dm, tc)]],
                    fence=False))
            tok_y_tc[tc] = tok_y[-1]

        al.free("gate", "u_nat", "hs")
        ao_sb = al.take("ao", [128, 16, D], BF16, side="top")
        tok_ao = []
        tok_ao_tt = {}
        for tt in range(16):
            for hc in range(2):
                par = (tt * 2 + hc) % 8
                ps = PS[:, par * 512:(par + 1) * 512]
                dep = [tok_y_tc[tt // 4], tok_wo]
                if par in (2, 3):
                    dep.append(tok_y[-1])
                if len(tok_ao) >= 8:
                    dep.append(tok_ao[-8])
                last = None
                for dm in range(8):
                    last = p.op("tensor",
                        lambda e, tt=tt, hc=hc, dm=dm, ps=ps: e.matmul(
                            ps, yT_sb[:, dm, tt * 128:(tt + 1) * 128],
                            wo_sb[:, dm, hc * 512:(hc + 1) * 512],
                            start=(dm == 0), stop=(dm == 7)), dep)
                tok_ao.append(p.op("scalar",
                    lambda e, tt=tt, hc=hc, ps=ps: e.activation(
                        out=ao_sb[:, tt, hc * 512:(hc + 1) * 512],
                        in_=ps, func=AF.Copy, bias=0.0, scale=1.0),
                    [last, tok_y[-1]], fence=False))
        ln2_prev = ln_chain[0]
        for tt in range(16):
            tok_ao_tt[tt] = p.op("vector", lambda e, tt=tt: e.tensor_tensor(
                out=x_sb[:, tt, :], in0=x_sb[:, tt, :], in1=ao_sb[:, tt, :],
                op=ALU.add), [tok_ao[2 * tt + 1], tok_xb[-1]],
                fence=(tt == 0))
            emit_ln_stats(x_sb, tt, tok_ao_tt[tt],
                          ln2_prev if tt == 0 else None, fence=True)
        tok_ao.append(tok_ao_tt[15])
        if debug:
            d = dbg_tap("x1", [128, 16 * D])
            p.dma("sync", lambda e, d=d: e.dma_start(
                out=d, in_=x_sb.rearrange("p a b -> p (a b)")), [tok_ao[-1]], sem="out")

        al.free("yT", "wo", "ctw", "dtw", "boutb", "ao")

        wup = [al.take("wup0", [128, 2, 8, MOE_I], BF16), None]
        wdn = [al.take("wdn0", [128, 2, 4, D], BF16), None]
        cb = [al.take("cb0", [128, 2, T_LOC], BF16), None]
        hr = [al.take("hr0", [128, 4, 256], BF16),
              al.take("hr1", [128, 4, 256], BF16)]
        hsq = [al.take("hsq0", [128, 4, 256], BF16),
               al.take("hsq1", [128, 4, 256], BF16)]

        w_toks = {}
        w_war = {}
        cb_toks = {}
        cbf_toks = {}
        tok_cscr0_box = [None]
        cb_war = {}
        tok_cscr_box = [None]

        def load_group_weights(g, extra_dep=None):
            par = g % 2
            dep = [w_war.get(g - 2), extra_dep]
            p.dma("gpsimd", lambda e, g=g, par=par: e.dma_start(
                out=wup[par],
                in_=UPT[2 * g:2 * g + 2].rearrange(
                    "e (k p) j -> p e k j", p=128)), dep, sem=f"w{par}")
            w_toks[g] = p.dma("gpsimd", lambda e, g=g, par=par: e.dma_start(
                out=wdn[par],
                in_=DOWNT[2 * g:2 * g + 2].rearrange(
                    "e (k p) j -> p e k j", p=128)), dep, sem=f"w{par}")

        def load_group_cb(g):
            par = g % 2
            if g < 2:
                cbf_toks[g] = p.dma("sync", lambda e, g=g, par=par: e.dma_start(
                    out=cb[par][:, :, 0:256],
                    in_=CSCR[2 * g:2 * g + 2, 0:256].unsqueeze(0)
                    .to_broadcast([128, 2, 256])),
                    [tok_cscr0_box[0], cb_war.get(g - 2)], sem=f"cf{par}")
                cb_toks[g] = p.dma("sync", lambda e, g=g, par=par: e.dma_start(
                    out=cb[par][:, :, 256:T_LOC],
                    in_=CSCR[2 * g:2 * g + 2, 256:T_LOC].unsqueeze(0)
                    .to_broadcast([128, 2, T_LOC - 256])),
                    [tok_cscr_box[0], cb_war.get(g - 2)], sem=f"cb{par}")
            else:
                cb_toks[g] = p.dma("sync", lambda e, g=g, par=par: e.dma_start(
                    out=cb[par],
                    in_=CSCR[2 * g:2 * g + 2, :].unsqueeze(0).to_broadcast(
                        [128, 2, T_LOC])),
                    [tok_cscr0_box[0], tok_cscr_box[0], cb_war.get(g - 2)],
                    sem=f"cb{par}")

        load_group_weights(0, extra_dep=tok_ao[-1])

        xn2_sb = al.take("xn2", [128, 16, D], BF16, side="top")
        xn2T_sb = al.take("xn2T", [128, 8, T_LOC], BF16)
        ln2_toks = emit_ln(x_sb, xn2_sb, 16, tok_ao[-1],
                           per_tile_dep=tok_ao_tt, norm_eng="mixed",
                           stats_done=True)
        tr2_toks = emit_transpose(xn2_sb, xn2T_sb, 16, ln2_toks)
        tok_xn2T = tr2_toks[-1]

        p.barrier()
        al.free("xn2")
        wup[1] = al.take("wup1", [128, 2, 8, MOE_I], BF16)
        wdn[1] = al.take("wdn1", [128, 2, 4, D], BF16)
        cb[1] = al.take("cb1", [128, 2, T_LOC], BF16)
        load_group_weights(1)

        scores_sb = al.take("scores", [128, 16, NEXP], F32)
        sc_sb = al.take("sc", [128, 16, NEXP], F32)
        masked_sb = al.take("masked", [128, 16, NEXP], F32)
        gm8_sb = tmp8_sb
        gk_sb = sel_all_sb[:, :, 0:8]
        gmask_sb = tmp8_sb
        max8_sb = tmp8_sb
        comb_sb = al.take("comb", [128, 16, NEXP], F32)
        combb_sb = al.take("combb", [128, 16, NEXP], BF16)
        knock_sb = comb_sb
        lgT_sb = al.take("lgT", [32, 2, 512], F32)
        identf_sb = al.take("identf", [32, 32], F32)
        tok_idf = p.dma("sync", lambda e: e.dma_start(
            out=identf_sb, in_=IDENT[0:32, 0:32]), sem="cscr")
        tok_scores = None
        for half in range(2):
            lg_evacs = {}
            for tc in (2 * half, 2 * half + 1):
                slot = tc % 2
                ps = PS[0:32, 2048 + slot * 512:2560 + slot * 512]
                last = None
                for hk in range(8):
                    last = p.op("tensor",
                        lambda e, tc=tc, hk=hk, ps=ps: e.matmul(
                            ps, gwt_sb[:, hk, :],
                            xn2T_sb[:, hk, tc * 512:(tc + 1) * 512],
                            start=(hk == 0), stop=(hk == 7)),
                        [tok_xn2T, tok_gwt])
                lg_evacs[slot] = p.op("scalar",
                    lambda e, slot=slot, ps=ps: e.activation(
                        out=lgT_sb[:, slot, :], in_=ps,
                        func=AF.Copy, bias=0.0, scale=1.0), [last],
                    fence=False)
            ps2 = PS[:, half * 512: half * 512 + 256].rearrange(
                "p (a b) -> p a b", b=NEXP)
            last = None
            for j in range(8):
                tt = half * 8 + j
                slot = (tt // 4) % 2
                last = p.op("tensor",
                    lambda e, j=j, tt=tt, slot=slot, ps2=ps2: e.matmul(
                        ps2[:, j, :],
                        lgT_sb[:, slot, (tt % 4) * 128:(tt % 4 + 1) * 128],
                        identf_sb, start=True, stop=True),
                    [lg_evacs[slot], tok_idf])
            tok_scores = p.op("scalar", lambda e, half=half, ps2=ps2: e.activation(
                out=scores_sb[:, half * 8:(half + 1) * 8, :].rearrange(
                    "p a b -> p (a b)"),
                in_=ps2.rearrange("p a b -> p (a b)"),
                func=AF.Sigmoid, bias=0.0, scale=1.0), [last], fence=False)
        tok_sc = p.op("vector", lambda e: e.tensor_tensor(
            out=sc_sb, in0=scores_sb,
            in1=ebias_sb.unsqueeze(1).to_broadcast([128, 16, NEXP]),
            op=ALU.add), [tok_scores, tok_ebias])
        g4 = sc_sb.rearrange("p a (g m) -> p a g m", m=4)
        pairs = [(0, 1), (0, 2), (0, 3), (1, 2), (1, 3), (2, 3)]
        tok_gs = p.op("vector", lambda e: e.tensor_tensor(
            out=gs_sb, in0=g4[:, :, :, 0], in1=g4[:, :, :, 1], op=ALU.add),
            [tok_sc])
        for (i, j) in pairs[1:]:
            t_a = p.op("vector", lambda e, i=i, j=j: e.tensor_tensor(
                out=tmp8_sb, in0=g4[:, :, :, i], in1=g4[:, :, :, j],
                op=ALU.add), [tok_sc])
            tok_gs = p.op("vector", lambda e: e.tensor_tensor(
                out=gs_sb, in0=gs_sb, in1=tmp8_sb, op=ALU.max), [t_a])
        for tt in range(16):
            p.op("vector", lambda e, tt=tt: e.max(
                out=gm8_sb[:, tt, :], in_=gs_sb[:, tt, :]), [],
                fence=(tt == 0))
        p.op("vector", lambda e: e.memset(gm8_sb[:, :, 4:8], NEG_SENT), [])
        for tt in range(16):
            p.op("vector", lambda e, tt=tt: e.match_replace(
                out=gk_sb[:, tt, :], in_to_replace=gm8_sb[:, tt, :],
                in_values=gs_sb[:, tt, :], imm_value=KNOCK), [],
                fence=(tt == 0))
        p.op("vector", lambda e: e.tensor_single_scalar(
            out=gmask_sb, in_=gk_sb,
            scalar=KNOCK, op=ALU.is_equal), [])
        p.op("vector", lambda e: e.tensor_tensor(
            out=masked_sb.rearrange("p a (g m) -> p (a g) m", m=4),
            in0=sc_sb.rearrange("p a (g m) -> p (a g) m", m=4),
            in1=gmask_sb.rearrange("p a b -> p (a b)").unsqueeze(2)
                .to_broadcast([128, 128, 4]),
            op=ALU.mult), [])
        for tt in range(16):
            p.op("vector", lambda e, tt=tt: e.max(
                out=max8_sb[:, tt, :], in_=masked_sb[:, tt, :]), [],
                fence=(tt == 0))
        p.op("vector", lambda e: e.memset(max8_sb[:, :, 6:8], NEG_SENT), [])
        for tt in range(16):
            p.op("vector", lambda e, tt=tt: e.match_replace(
                out=knock_sb[:, tt, :], in_to_replace=max8_sb[:, tt, :],
                in_values=masked_sb[:, tt, :], imm_value=KNOCK), [],
                fence=(tt == 0))
        tok_comb = p.op("vector", lambda e: e.tensor_single_scalar(
            out=sel_all_sb.rearrange("p a b -> p (a b)"),
            in_=knock_sb.rearrange("p a b -> p (a b)"),
            scalar=KNOCK, op=ALU.is_equal), [])
        p.op("vector", lambda e: e.tensor_tensor(
            out=comb_sb.rearrange("p a b -> p (a b)"),
            in0=sel_all_sb.rearrange("p a b -> p (a b)"),
            in1=scores_sb.rearrange("p a b -> p (a b)"), op=ALU.mult), [])
        p.op("vector", lambda e: e.tensor_reduce(
            out=rden_sb, in_=comb_sb, axis=AX.X, op=ALU.add), [])
        p.op("vector", lambda e: e.tensor_single_scalar(
            out=rden_sb, in_=rden_sb, scalar=1e-20, op=ALU.add), [])
        p.op("vector", lambda e: e.reciprocal(out=rden_sb, in_=rden_sb), [])
        p.op("vector", lambda e: e.tensor_single_scalar(
            out=rden_sb, in_=rden_sb, scalar=2.5, op=ALU.mult), [])
        tok_comb = p.op("vector", lambda e: e.tensor_tensor(
            out=comb_sb, in0=comb_sb,
            in1=rden_sb.unsqueeze(2).to_broadcast([128, 16, NEXP]),
            op=ALU.mult), [])
        tok_combb = p.op("vector", lambda e: e.tensor_copy(
            out=combb_sb.rearrange("p a b -> p (a b)"),
            in_=comb_sb.rearrange("p a b -> p (a b)")), [tok_comb])

        NG = 16
        ps_up = [PS[:, 0:1024].rearrange("p (m n) -> p m n", m=4),
                 PS[:, 1024:2048].rearrange("p (m n) -> p m n", m=4)]
        ps_acc = PS[:, 2048:4096].rearrange("p (m n) -> p m n", m=2)
        prev_relu = {}
        prev_dn = {}
        acc_evac_last = [None]
        early_sh_up = {}
        shup_sb = None
        tok_shup = None
        pre_tq = {}
        pre_lastup = {}

        def moe_ups(g, tb, ei):
            par = g % 2
            upar = ei
            dep = [w_toks[g], tok_xn2T, prev_relu.get(upar), tok_scores]
            last_up = None
            for mi in range(4):
                for hk in range(8):
                    last_up = p.op("tensor",
                        lambda e, par=par, ei=ei, mi=mi, hk=hk, tb=tb,
                        upar=upar: e.matmul(
                            ps_up[upar][:, mi, :],
                            wup[par][:, ei, hk, mi * 128:(mi + 1) * 128],
                            xn2T_sb[:, hk, tb * 256:(tb + 1) * 256],
                            start=(hk == 0), stop=(hk == 7)), dep)
            return last_up

        def moe_relu_sq(g, tb, ei, last_up):
            upar = ei
            t_r = p.op("scalar", lambda e, upar=upar: e.activation(
                out=hr[upar].rearrange("p m n -> p (m n)"),
                in_=ps_up[upar].rearrange("p m n -> p (m n)"),
                func=AF.Relu, bias=0.0, scale=1.0),
                [last_up, prev_dn.get(upar)], fence=False)
            prev_relu[upar] = t_r
            return p.op("vector", lambda e, upar=upar: e.tensor_tensor(
                out=hsq[upar].rearrange("p m n -> p (m n)"),
                in0=hr[upar].rearrange("p m n -> p (m n)"),
                in1=hr[upar].rearrange("p m n -> p (m n)"),
                op=ALU.mult), [t_r], fence=False)

        for ei in range(2):
            pre_tq[(0, ei)] = moe_relu_sq(0, 0, ei, moe_ups(0, 0, ei))
        for ei in range(2):
            pre_lastup[(1, ei)] = moe_ups(0, 1, ei)

        tok_ctr = []
        for grp in range(4):
            par = grp % 2
            ps = PS[0:32, 2048 + par * 512:2560 + par * 512]
            dep = [tok_combb, tok_ident, tok_ao[-1]]
            if len(tok_ctr) >= 2:
                dep.append(tok_ctr[-2])
            last = None
            for j in range(4):
                tt = grp * 4 + j
                last = p.op("tensor", lambda e, j=j, tt=tt, ps=ps: e.matmul(
                    ps[:, j * 128:(j + 1) * 128],
                    combb_sb[:, tt, :], ident_sb, start=True, stop=True), dep)
            tok_ctr.append(p.op("scalar", lambda e, grp=grp, ps=ps: e.activation(
                out=ct_sb[:, grp * 4:(grp + 1) * 4, :].rearrange(
                    "q a b -> q (a b)"),
                in_=ps, func=AF.Copy, bias=0.0, scale=1.0), [last],
                fence=False))
        tok_cscr0 = p.dma("sync", lambda e: e.dma_start(
            out=CSCR[:, 0:256],
            in_=ct_sb[:, 0:2, :].rearrange("q a b -> q (a b)")),
            [tok_ctr[0]], sem="cscr")
        tok_cscr1 = p.dma("sync", lambda e: e.dma_start(
            out=CSCR[:, 256:T_LOC],
            in_=ct_sb[:, 2:16, :].rearrange("q a b -> q (a b)")),
            [tok_ctr[-1]], sem="c2")
        tok_cscr_box[0] = tok_cscr1
        tok_cscr0_box[0] = tok_cscr0
        if debug:
            d = dbg_tap("combine", [128, 16 * NEXP])
            p.dma("sync", lambda e, d=d: e.dma_start(
                out=d, in_=comb_sb.rearrange("p a b -> p (a b)")), [tok_comb], sem="out")

        load_group_cb(0)
        load_group_cb(1)
        al.free("scores", "sc", "masked", "comb", "combb", "cT", "lgT",
                "identf")

        for g in range(NG):
            par = g % 2
            last_dn = None
            for tb in range(8):
                t_hs = {}
                for ei in range(2):
                    upar = ei
                    if g == 0 and (tb, ei) in pre_tq:
                        t_q = pre_tq[(tb, ei)]
                    else:
                        if g == 0 and (tb, ei) in pre_lastup:
                            last_up = pre_lastup[(tb, ei)]
                        else:
                            last_up = moe_ups(g, tb, ei)
                        t_q = moe_relu_sq(g, tb, ei, last_up)
                    cb_dep = (cbf_toks[g] if g < 2 and tb == 0
                              else cb_toks[g])
                    t_hs[ei] = p.op("vector",
                        lambda e, par=par, ei=ei, tb=tb, upar=upar: e.tensor_tensor(
                            out=hr[upar],
                            in0=hsq[upar],
                            in1=cb[par][:, ei, tb * 256:(tb + 1) * 256]
                                .unsqueeze(1).to_broadcast([128, 4, 256]),
                            op=ALU.mult),
                        [t_q, cb_dep, prev_dn.get(upar)])
                if g == NG - 1 and tb == 7:
                    for q in (0, 1):
                        dep = [tok_shup, tok_xn2T, prev_relu[q]]
                        lu = None
                        for mi in range(4):
                            for hk in range(8):
                                lu = p.op("tensor",
                                    lambda e, q=q, mi=mi, hk=hk: e.matmul(
                                        PS[:, q * 1024:(q + 1) * 1024]
                                        .rearrange("p (m n) -> p m n",
                                                   m=4)[:, mi, :],
                                        shup_sb[:, hk,
                                                q * 512 + mi * 128:
                                                q * 512 + (mi + 1) * 128],
                                        xn2T_sb[:, hk, 0:256],
                                        start=(hk == 0), stop=(hk == 7)), dep)
                        early_sh_up[q] = lu
                for ei in range(2):
                    upar = ei
                    depd = [t_hs[ei], w_toks[g], tok_ao[-1], tok_ctr[-1]]
                    if ei == 0 and acc_evac_last[0] is not None:
                        depd.append(acc_evac_last[0])
                    for ki in range(4):
                        for mt in range(2):
                            for nh in range(2):
                                last_dn = p.op("tensor",
                                    lambda e, par=par, ei=ei, ki=ki, mt=mt,
                                    nh=nh, upar=upar: e.matmul(
                                        ps_acc[:, mt, nh * 512:(nh + 1) * 512],
                                        hr[upar][:, ki, mt * 128:(mt + 1) * 128],
                                        wdn[par][:, ei, ki,
                                                 nh * 512:(nh + 1) * 512],
                                        start=(ei == 0 and ki == 0),
                                        stop=(ei == 1 and ki == 3)), depd)
                    prev_dn[upar] = last_dn
                evs = None
                for mt in range(2):
                    tt = tb * 2 + mt
                    evs = p.op("vector",
                        lambda e, mt=mt, tt=tt: e.scalar_tensor_tensor(
                            out=x_sb[:, tt, :], in0=ps_acc[:, mt, :],
                            scalar=1.0, in1=x_sb[:, tt, :],
                            op0=ALU.mult, op1=ALU.add),
                        [last_dn, ln2_toks[tt]], fence=False)
                acc_evac_last[0] = evs
            w_war[g] = last_dn
            cb_war[g] = last_dn
            if g + 2 < NG:
                load_group_weights(g + 2)
                load_group_cb(g + 2)
            if g == NG - 2:
                al.free("wup0", "wdn0", "cb0")
                shup_sb = al.take("shup", [128, 8, SH_I], BF16)
                tok_shup = p.dma("gpsimd", lambda e: e.dma_start(
                    out=shup_sb,
                    in_=SHUPT.rearrange("(k p) j -> p k j", p=128)),
                    [w_war[g], cb_war[g]], sem="sh")
                shdn_a = al.take("shdna", [128, 4, D], BF16)
                tok_shdn_a = p.dma("gpsimd", lambda e: e.dma_start(
                    out=shdn_a,
                    in_=SHDOWNT[0:512, :].rearrange("(k p) j -> p k j", p=128)),
                    [w_war[g], cb_war[g]], sem="sh2")

        s1 = [al.take("s10", [128, 4, 256], BF16),
              al.take("s11", [128, 4, 256], BF16)]
        al.free("wup1", "wdn1", "cb1", "hr0", "hr1", "hsq0", "hsq1")
        shdn_b = al.take("shdnb", [128, 12, D], BF16)
        tok_shdn_b = p.dma("gpsimd", lambda e: e.dma_start(
            out=shdn_b, in_=SHDOWNT[512:2048, :].rearrange(
                "(k p) j -> p k j", p=128)),
            [w_war[NG - 1], cb_war[NG - 1]], sem="sh3")
        shdn_row = lambda r: shdn_a[:, r, :] if r < 4 else shdn_b[:, r - 4, :]

        ps_s1 = [PS[:, 0:1024].rearrange("p (m n) -> p m n", m=4),
                 PS[:, 1024:2048].rearrange("p (m n) -> p m n", m=4)]
        tok_out = []
        sprev_relu = dict(prev_relu)
        sprev_dn = {}
        last_acc2 = acc_evac_last[0]
        for tb in range(8):
            last_dn = None
            for qp in (0, 2):
                t_qs = {}
                for q in (qp, qp + 1):
                    spar = q % 2
                    if tb == 0 and q in early_sh_up:
                        last_up = early_sh_up[q]
                    else:
                        dep = [tok_shup, tok_xn2T, sprev_relu.get(spar)]
                        last_up = None
                        for mi in range(4):
                            for hk in range(8):
                                last_up = p.op("tensor",
                                    lambda e, q=q, mi=mi, hk=hk, tb=tb,
                                    spar=spar: e.matmul(
                                        ps_s1[spar][:, mi, :],
                                        shup_sb[:, hk,
                                                q * 512 + mi * 128:
                                                q * 512 + (mi + 1) * 128],
                                        xn2T_sb[:, hk,
                                                tb * 256:(tb + 1) * 256],
                                        start=(hk == 0), stop=(hk == 7)), dep)
                    t_r = p.op("scalar", lambda e, spar=spar: e.activation(
                        out=s1[spar].rearrange("p m n -> p (m n)"),
                        in_=ps_s1[spar].rearrange("p m n -> p (m n)"),
                        func=AF.Relu, bias=0.0, scale=1.0),
                        [last_up, sprev_dn.get(spar)], fence=False)
                    sprev_relu[spar] = t_r
                    t_qs[q] = p.op("vector", lambda e, spar=spar: e.tensor_tensor(
                        out=s1[spar].rearrange("p m n -> p (m n)"),
                        in0=s1[spar].rearrange("p m n -> p (m n)"),
                        in1=s1[spar].rearrange("p m n -> p (m n)"),
                        op=ALU.mult), [t_r], fence=False)
                for q in (qp, qp + 1):
                    spar = q % 2
                    depd = [t_qs[q], tok_shdn_a if q == 0 else tok_shdn_b]
                    if q == 0 and last_acc2 is not None:
                        depd.append(last_acc2)
                    for ki in range(4):
                        for mt in range(2):
                            for nh in range(2):
                                last_dn = p.op("tensor",
                                    lambda e, q=q, ki=ki, mt=mt, nh=nh,
                                    spar=spar: e.matmul(
                                        ps_acc[:, mt, nh * 512:(nh + 1) * 512],
                                        s1[spar][:, ki, mt * 128:(mt + 1) * 128],
                                        shdn_row(q * 4 + ki)[:,
                                                nh * 512:(nh + 1) * 512],
                                        start=(q == 0 and ki == 0),
                                        stop=(q == 3 and ki == 3)), depd)
                    sprev_dn[spar] = last_dn
            evs = None
            for mt in range(2):
                tt = tb * 2 + mt
                evs = p.op("vector",
                    lambda e, mt=mt, tt=tt: e.scalar_tensor_tensor(
                        out=x_sb[:, tt, :], in0=ps_acc[:, mt, :], scalar=1.0,
                        in1=x_sb[:, tt, :], op0=ALU.mult, op1=ALU.add),
                    [last_dn], fence=False)
            last_acc2 = evs
            tok_out.append(p.dma("sync", lambda e, tb=tb: e.dma_start(
                out=OUT.rearrange("(a p) h -> p a h", p=128)[:,
                    tb * 2:tb * 2 + 2, :],
                in_=x_sb[:, tb * 2:tb * 2 + 2, :]), [evs], sem="out"))

        p.final_waits = [tok_out[-1]]

        p.emit(block, sems)

    return nc, dbg_taps


def host_inputs(inputs):
    w = {
        "W_IN": np.ascontiguousarray(np.asarray(inputs["L0_attn_W_in"], np.float32)),
        "W_GATE": np.ascontiguousarray(
            np.asarray(inputs["L0_attn_W_gate"], np.float32)),
        "W_OUT": np.ascontiguousarray(
            np.asarray(inputs["L0_attn_W_out"], np.float32)),
        "CT": np.ascontiguousarray(np.asarray(inputs["L0_attn_C"], np.float32).T),
        "DT": np.ascontiguousarray(np.asarray(inputs["L0_attn_D"], np.float32).T),
        "A_N": np.ascontiguousarray(np.asarray(inputs["L0_attn_A"], np.float32)),
        "A_T": np.ascontiguousarray(np.asarray(inputs["L0_attn_A"], np.float32).T),
        "BM_N": np.ascontiguousarray(np.asarray(inputs["L0_attn_Bm"], np.float32)),
        "BM_T": np.ascontiguousarray(
            np.asarray(inputs["L0_attn_Bm"], np.float32).T),
        "B_IN": np.ascontiguousarray(
            np.asarray(inputs["L0_attn_b_in"], np.float32).reshape(2, 128).T),
        "B_GATE": np.ascontiguousarray(
            np.asarray(inputs["L0_attn_b_gate"], np.float32).reshape(8, 128).T),
        "B_OUT": np.ascontiguousarray(
            np.asarray(inputs["L0_attn_b_out"], np.float32).reshape(1, D)),
        "E_BIAS": np.ascontiguousarray(
            np.asarray(inputs["L0_ffn_e_bias"], np.float32).reshape(1, NEXP)),
        "GWT": np.ascontiguousarray(
            np.asarray(inputs["L0_ffn_gate_w"], np.float32).T),
        "UPT": np.ascontiguousarray(
            np.asarray(inputs["L0_ffn_up_w"], np.float32).transpose(0, 2, 1)),
        "DOWNT": np.ascontiguousarray(
            np.asarray(inputs["L0_ffn_down_w"], np.float32).transpose(0, 2, 1)),
        "SHUPT": np.ascontiguousarray(
            np.asarray(inputs["L0_ffn_shared_up_w"], np.float32).T),
        "SHDOWNT": np.ascontiguousarray(
            np.asarray(inputs["L0_ffn_shared_down_w"], np.float32).T),
        "IDENT": np.eye(128, dtype=np.float32),
    }
    x_full = np.ascontiguousarray(np.asarray(inputs["L0_attn_x"], np.float32))
    in_maps = []
    for core in range(8):
        b, half = core // 2, core % 2
        m = dict(w)
        m["X"] = np.ascontiguousarray(x_full[b, half * T_LOC:(half + 1) * T_LOC, :])
        m["XPREV"] = (np.ascontiguousarray(x_full[b, 0:T_LOC, :]) if half == 1
                      else np.zeros((T_LOC, D), np.float32))
        m["FLAG"] = np.full((1, 1), 1.0 if half == 1 else 0.0, np.float32)
        in_maps.append(m)
    return x_full, in_maps


def kernel(**inputs):
    x_full, in_maps = host_inputs(inputs)
    B, T, _ = x_full.shape
    nc, _ = build_program(debug=False)
    res = run_bass_kernel_spmd(nc, in_maps, core_ids=list(range(8)))
    out = np.empty((B, T, D), dtype=np.float32)
    for core in range(8):
        b, half = core // 2, core % 2
        out[b, half * T_LOC:(half + 1) * T_LOC, :] = res.results[core]["OUT"]
    return out
